# version 02
# baseline (speedup 1.0000x reference)
import contextlib

import numpy as np

import concourse.bass as bass
import concourse.mybir as mybir
from concourse.bass_utils import run_bass_kernel_spmd

F32 = mybir.dt.float32
BF16 = mybir.dt.bfloat16
AF = mybir.ActivationFunctionType
ALU = mybir.AluOpType
AX = mybir.AxisListType

NCORES = 8
B, T, D = 2, 4096, 2048
TOK = B * T
TPC = TOK // NCORES
NT = TPC // 128
DCH = D // 128
DFF = 8192

C32 = 1.5 * 2.0 ** 23
C64 = 1.5 * 2.0 ** 52
LN_EPS = 1e-5


class Tok:
    __slots__ = ("kind", "eng", "sem", "val", "issue_tok")

    def __init__(self, kind, eng, sem, val):
        self.kind = kind
        self.eng = eng
        self.sem = sem
        self.val = val
        self.issue_tok = None


class Sched:

    ENGS = ("sync", "scalar", "vector", "tensor", "gpsimd")

    def __init__(self, nc):
        self.nc = nc
        self.items = {e: [] for e in self.ENGS}
        self.opcount = {e: 0 for e in self.ENGS}
        self.dmacount = {}
        self.last_dma = {}
        self.sem_names = set()

    def op(self, eng, fn, deps=()):
        self.opcount[eng] += 1
        t = Tok("op", eng, None, self.opcount[eng])
        self.items[eng].append((fn, tuple(d for d in deps if d is not None), t))
        return t

    def dma(self, eng, semname, fn, deps=()):
        self.sem_names.add(semname)
        self.dmacount[semname] = self.dmacount.get(semname, 0) + 1
        t = Tok("dma", eng, semname, self.dmacount[semname])
        self.opcount[eng] += 1
        t.issue_tok = Tok("op", eng, None, self.opcount[eng])
        deps = tuple(d for d in deps if d is not None)
        if semname in self.last_dma:
            deps = deps + (self.last_dma[semname],)
        self.last_dma[semname] = t
        self.items[eng].append((fn, deps, t))
        return t

    def wait(self, eng, deps):
        self.items[eng].append((None, tuple(d for d in deps if d is not None), None))

    def finish(self, stack):
        for e in ("scalar", "vector", "gpsimd"):
            prev = None
            newitems = []
            for fn, deps, t in self.items[e]:
                is_compute = t is not None and t.kind == "op" and fn is not None
                if prev is not None and is_compute:
                    deps = deps + (prev,)
                if is_compute:
                    prev = t
                newitems.append((fn, deps, t))
            self.items[e] = newitems
        needed = set()
        for e in self.ENGS:
            for fn, deps, t in self.items[e]:
                for d in deps:
                    if d.kind == "op":
                        needed.add(id(d))
        sems = {}
        for e in self.ENGS:
            sems["__eng_" + e] = stack.enter_context(self.nc.semaphore("se_" + e))
        for n in sorted(self.sem_names):
            sems[n] = stack.enter_context(self.nc.semaphore(n))
        prodval = {}
        for e in self.ENGS:
            cnt = 0
            for fn, deps, t in self.items[e]:
                if t is None:
                    continue
                key = t if t.kind == "op" else t.issue_tok
                if id(key) in needed:
                    cnt += 1
                    prodval[id(key)] = cnt

        def run_engine(engobj, ename):
            for fn, deps, t in self.items[ename]:
                for d in deps:
                    if d.kind == "dma":
                        engobj.wait_ge(sems[d.sem], 16 * d.val)
                    else:
                        engobj.wait_ge(sems["__eng_" + d.eng], prodval[id(d)])
                if fn is None:
                    continue
                inst = fn(engobj)
                if t.kind == "dma":
                    inst = inst.then_inc(sems[t.sem], 16)
                    if id(t.issue_tok) in needed:
                        inst.then_inc(sems["__eng_" + t.eng], 1)
                elif id(t) in needed:
                    inst.then_inc(sems["__eng_" + t.eng], 1)

        with self.nc.Block() as block:
            for ename in self.ENGS:
                if self.items[ename]:
                    getattr(block, ename)(
                        lambda engobj, _e=ename: run_engine(engobj, _e))



def emit_ln(S, x_ap, xn_ap, st6, st2, rstd, mbias, deps, eps=LN_EPS):
    ncols = x_ap.shape[-1]
    nst = (ncols + 511) // 512
    t1s = []
    for i in range(nst):
        c0, c1 = i * 512, min((i + 1) * 512, ncols)
        t1s.append(S.op("vector", lambda v, i=i, c0=c0, c1=c1: v.bn_stats(
            st6[:, 6 * i:6 * (i + 1)], x_ap[:, c0:c1]), deps))
    t2 = S.op("vector", lambda v: v.bn_aggr(st2, st6[:, 0:6 * nst]), t1s)
    t3 = S.op("vector", lambda v: v.tensor_scalar(
        out=rstd, in0=st2[:, 1:2], scalar1=eps, scalar2=None, op0=ALU.add), [t2])
    t4 = S.op("scalar", lambda s: s.activation(out=rstd, in_=rstd, func=AF.Sqrt), [t3])
    t5 = S.op("vector", lambda v: v.reciprocal(out=rstd, in_=rstd), [t4])
    t6 = S.op("vector", lambda v: v.scalar_tensor_tensor(
        out=mbias, in0=st2[:, 0:1], scalar=-1.0, in1=rstd,
        op0=ALU.mult, op1=ALU.mult), [t5])
    t7 = S.op("scalar", lambda s: s.activation(
        out=xn_ap, in_=x_ap, func=AF.Identity, bias=mbias, scale=rstd),
        [t6] + list(deps))
    return t7, t5



def magic_round(S, out, tmp, u, m_ap, deps):
    t1 = S.op("vector", lambda v: v.tensor_scalar(
        out=out, in0=u, scalar1=C32, scalar2=C32, op0=ALU.add, op1=ALU.subtract), deps)
    t2 = S.op("vector", lambda v: v.tensor_scalar(
        out=tmp, in0=u, scalar1=C64, scalar2=C64, op0=ALU.add, op1=ALU.subtract), deps)
    t3 = S.op("vector", lambda v: v.tensor_tensor(
        out=tmp, in0=tmp, in1=out, op=ALU.subtract), [t1, t2])
    t4 = S.op("vector", lambda v: v.scalar_tensor_tensor(
        out=out, in0=tmp, scalar=m_ap, in1=out, op0=ALU.mult, op1=ALU.add), [t3])
    return t4


def emit_mflag(S, m_tile, deps=()):
    t1 = S.op("vector", lambda v: v.memset(m_tile, 1.0), deps)
    t2 = S.op("vector", lambda v: v.tensor_scalar(
        out=m_tile, in0=m_tile, scalar1=2.0 ** -40, scalar2=1.0,
        op0=ALU.add, op1=ALU.subtract), [t1])
    t3 = S.op("vector", lambda v: v.tensor_scalar(
        out=m_tile, in0=m_tile, scalar1=2.0 ** 40, scalar2=None, op0=ALU.mult), [t2])
    return t3


def emit_identity(S, ones, ident):
    o1 = S.op("vector", lambda v: v.memset(ones, 1.0))
    t = S.op("gpsimd", lambda g: g.affine_select(
        out=ident, in_=ones, pattern=[[-1, 128]], compare_op=ALU.is_equal,
        fill=0.0, base=0, channel_multiplier=1), [o1])
    return t


class PsumBanks:

    def __init__(self, n):
        self.last = [None] * n

    def use(self, i, newlast):
        self.last[i] = newlast



def build_qscale():
    nc = bass.Bass("TRN2", target_bir_lowering=False)
    w1 = nc.dram_tensor("w1s", [256, 8192], F32, kind="ExternalInput").ap()
    up = nc.dram_tensor("ups", [256, 8192], F32, kind="ExternalInput").ap()
    w2 = nc.dram_tensor("w2s", [1024, 2048], F32, kind="ExternalInput").ap()
    dn = nc.dram_tensor("dns", [1024, 2048], F32, kind="ExternalInput").ap()
    out = nc.dram_tensor("sums", [1, 4], F32, kind="ExternalOutput").ap()

    S = Sched(nc)
    with contextlib.ExitStack() as stack:
        sb = lambda n, sh, dt=F32: stack.enter_context(nc.sbuf_tensor(n, sh, dt))
        buf = [sb("buf%d" % i, [128, 8192]) for i in range(2)]
        red = sb("red", [128, 4])
        acc = sb("acc", [128, 4])
        ones = sb("ones", [128, 1])
        stat = sb("stat", [1, 4])
        ps = stack.enter_context(nc.psum_tensor("ps", [1, 4], F32))

        z = S.op("vector", lambda v: v.memset(acc[:], 0.0))
        o1 = S.op("vector", lambda v: v.memset(ones[:], 1.0))
        mats = [(w1, 2, 8192, 0), (up, 2, 8192, 1), (w2, 8, 2048, 2), (dn, 8, 2048, 3)]
        prev = z
        last_use = [None, None]
        for w, ntiles, cols, col in mats:
            for i in range(ntiles):
                bi = i % 2
                ld = S.dma("sync", "ldw%d" % bi,
                           lambda e, w=w, i=i, bi=bi, cols=cols: e.dma_start(
                               out=buf[bi][:, 0:cols], in_=w[i * 128:(i + 1) * 128, :]),
                           deps=[last_use[bi]])
                r = S.op("vector", lambda v, bi=bi, cols=cols, col=col: v.reduce_sum(
                    out=red[:, col:col + 1], in_=buf[bi][:, 0:cols], axis=AX.X,
                    apply_absolute_value=True), [ld, prev])
                a = S.op("vector", lambda v, col=col: v.tensor_tensor(
                    out=acc[:, col:col + 1], in0=acc[:, col:col + 1],
                    in1=red[:, col:col + 1], op=ALU.add), [r])
                last_use[bi] = r
                prev = a
        mm = S.op("tensor", lambda t: t.matmul(ps[:, :], ones[:, 0:1], acc[:, :],
                                               start=True, stop=True), [prev, o1])
        ev = S.op("vector", lambda v: v.tensor_copy(stat[:, :], ps[:, :]), [mm])
        st = S.dma("sync", "sto", lambda e: e.dma_start(out=out[:, :], in_=stat[:, :]),
                   [ev])
        S.wait("sync", [st])
        S.finish(stack)
    return nc



def build_quant():
    nc = bass.Bass("TRN2", target_bir_lowering=False)
    w1 = nc.dram_tensor("w1s", [256, 8192], F32, kind="ExternalInput").ap()
    up = nc.dram_tensor("ups", [256, 8192], F32, kind="ExternalInput").ap()
    w2 = nc.dram_tensor("w2s", [1024, 2048], F32, kind="ExternalInput").ap()
    dn = nc.dram_tensor("dns", [1024, 2048], F32, kind="ExternalInput").ap()
    psums = nc.dram_tensor("psums", [8, 4], F32, kind="ExternalInput").ap()
    qw1 = nc.dram_tensor("qw1", [256, 8192], F32, kind="ExternalOutput").ap()
    qup = nc.dram_tensor("qup", [256, 8192], F32, kind="ExternalOutput").ap()
    qw2 = nc.dram_tensor("qw2", [1024, 2048], F32, kind="ExternalOutput").ap()
    qdn = nc.dram_tensor("qdn", [1024, 2048], F32, kind="ExternalOutput").ap()
    scl = nc.dram_tensor("scales", [1, 4], F32, kind="ExternalOutput").ap()

    S = Sched(nc)
    with contextlib.ExitStack() as stack:
        sb = lambda n, sh, dt=F32: stack.enter_context(nc.sbuf_tensor(n, sh, dt))
        buf = [sb("buf%d" % i, [128, 8192]) for i in range(2)]
        qb = [sb("qb%d" % i, [128, 8192]) for i in range(2)]
        tb = sb("tb", [128, 8192])
        psl = sb("psl", [8, 4])
        ones = sb("ones", [128, 1])
        sc1 = sb("sc1", [1, 4])
        inv = sb("inv", [128, 4])
        scs = sb("scs", [1, 4])
        pp = stack.enter_context(nc.psum_tensor("pp", [128, 4], F32))

        o1 = S.op("vector", lambda v: v.memset(ones[:], 1.0))
        lps = S.dma("sync", "lps", lambda e: e.dma_start(out=psl[:, :], in_=psums[:, :]))
        smm = S.op("tensor", lambda t: t.matmul(pp[0:1, :], ones[0:8, 0:1], psl[:, :],
                                                start=True, stop=True), [lps, o1])
        NEL = float(DFF * D)
        s1 = S.op("vector", lambda v: v.tensor_scalar(
            out=sc1[:, :], in0=pp[0:1, :], scalar1=1.0 / NEL, scalar2=1e-8,
            op0=ALU.mult, op1=ALU.max), [smm])
        sev = S.op("vector", lambda v: v.tensor_copy(scs[:, :], sc1[:, :]), [s1])
        sst = S.dma("sync", "sst", lambda e: e.dma_start(out=scl[:, :], in_=scs[:, :]),
                    [sev])
        s2 = S.op("vector", lambda v: v.reciprocal(out=sc1[:, :], in_=sc1[:, :]),
                  [s1, sev])
        onesr = sb("onesr", [1, 128])
        o2 = S.op("vector", lambda v: v.memset(onesr[:], 1.0))
        bmm = S.op("tensor", lambda t: t.matmul(pp[:, :], onesr[0:1, :], sc1[:, :],
                                                start=True, stop=True), [s2, o2, smm, s1])
        bev = S.op("vector", lambda v: v.tensor_copy(inv[:, :], pp[:, :]), [bmm])

        mats = [(w1, qw1, 2, 8192, 0), (up, qup, 2, 8192, 1),
                (w2, qw2, 8, 2048, 2), (dn, qdn, 8, 2048, 3)]
        last_q = [None, None]
        last_st = [None, None]
        for w, qw, ntiles, cols, col in mats:
            for i in range(ntiles):
                bi = i % 2
                ld = S.dma("sync", "lw%d" % bi,
                           lambda e, w=w, i=i, bi=bi, cols=cols: e.dma_start(
                               out=buf[bi][:, 0:cols], in_=w[i * 128:(i + 1) * 128, :]),
                           deps=[last_q[bi]])
                q1 = S.op("vector", lambda v, bi=bi, cols=cols, col=col: v.tensor_scalar(
                    out=tb[:, 0:cols], in0=buf[bi][:, 0:cols], scalar1=inv[:, col:col + 1],
                    scalar2=0.5, op0=ALU.mult, op1=ALU.is_gt), [ld, bev] + ([last_q[1 - bi]] if last_q[1 - bi] else []))
                q2 = S.op("vector", lambda v, bi=bi, cols=cols, col=col: v.tensor_scalar(
                    out=qb[bi][:, 0:cols], in0=buf[bi][:, 0:cols], scalar1=inv[:, col:col + 1],
                    scalar2=-0.5, op0=ALU.mult, op1=ALU.is_lt),
                    [ld, bev] + ([last_st[bi]] if last_st[bi] else []))
                q3 = S.op("vector", lambda v, bi=bi, cols=cols: v.tensor_tensor(
                    out=qb[bi][:, 0:cols], in0=tb[:, 0:cols], in1=qb[bi][:, 0:cols],
                    op=ALU.subtract), [q1, q2])
                last_q[bi] = q3
                st = S.dma("gpsimd", "sq%d" % bi,
                           lambda e, qw=qw, i=i, bi=bi, cols=cols: e.dma_start(
                               out=qw[i * 128:(i + 1) * 128, :], in_=qb[bi][:, 0:cols]),
                           deps=[q3])
                last_st[bi] = st
        S.wait("sync", [last_st[0], last_st[1], sst])
        S.finish(stack)
    return nc



def build_attn_a():
    nc = bass.Bass("TRN2", target_bir_lowering=False)
    NTH = NT + 1
    xh = nc.dram_tensor("xh", [NTH * 128, D], F32, kind="ExternalInput").ap()
    flag = nc.dram_tensor("flag", [128, 1], F32, kind="ExternalInput").ap()
    waT = nc.dram_tensor("waT", [D, 3 * D], F32, kind="ExternalInput").ap()
    woT = nc.dram_tensor("woT", [D, D], F32, kind="ExternalInput").ap()
    y = nc.dram_tensor("y", [TPC, D], F32, kind="ExternalOutput").ap()

    S = Sched(nc)
    with contextlib.ExitStack() as stack:
        sb = lambda n, sh, dt=F32: stack.enter_context(nc.sbuf_tensor(n, sh, dt))
        xt = sb("xt", [128, D])
        xn = sb("xn", [128, D], BF16)
        st6 = sb("st6", [128, 24]); st2 = sb("st2", [128, 2])
        rstd = sb("rstd", [128, 1]); mbias = sb("mbias", [128, 1])
        flg = sb("flg", [128, 1])
        ones = sb("ones", [128, 128], BF16)
        ident = sb("ident", [128, 128], BF16)
        bigA = sb("bigA", [128, DCH * NTH * 128], BF16)
        wsl = [sb("wsl%d" % i, [128, DCH * 256], BF16) for i in range(2)]
        qf = sb("qf", [128, DCH * TPC], BF16)
        kf = sb("kf", [128, DCH * NTH * 128], BF16)
        vt = sb("vt", [128, NTH * D], BF16)
        sev = sb("sevac", [128, 256])
        att = sb("att", [128, 256], BF16)
        ybuf = [sb("yb%d" % i, [128, 512]) for i in range(2)]
        xr = sb("xr", [128, D])
        ps_t = stack.enter_context(nc.psum_tensor("ps_t", [128, 2048], BF16))
        ps_a = stack.enter_context(nc.psum_tensor("ps_a", [128, 2048], F32))
        ps_s = stack.enter_context(nc.psum_tensor("ps_s", [128, 1024], F32))

        idt = emit_identity(S, ones[:], ident[:])
        lfl = S.dma("sync", "lf", lambda e: e.dma_start(out=flg[:], in_=flag[:, :]))

        tp_banklast = [None] * 4
        lnlast = None
        xn_war = None
        xt_war = None
        last_evs = None
        for i in range(NTH):
            ld = S.dma("sync", "lx%d" % (i % 2),
                       lambda e, i=i: e.dma_start(
                           out=xt[:], in_=xh[i * 128:(i + 1) * 128, :]),
                       deps=[xt_war])
            lnt, _ = emit_ln(S, xt[:], xn[:], st6[:], st2[:], rstd[:], mbias[:],
                             [ld, lnlast, xn_war])
            xt_war = lnt
            lnlast = lnt
            if i == 0:
                lnt = S.op("vector", lambda v: v.tensor_scalar(
                    out=xn[:], in0=xn[:], scalar1=flg[:, 0:1], scalar2=None,
                    op0=ALU.mult), [lnt, lfl])
            evs = []
            for c in range(DCH):
                bank = c % 2
                tr = S.op("tensor", lambda t, c=c, bank=bank: t.transpose(
                    ps_t[:, bank * 1024:bank * 1024 + 128], xn[:, c * 128:(c + 1) * 128],
                    ident[:]), [lnt, idt, tp_banklast[bank]])
                dst = bigA[:, (c * NTH + i) * 128:(c * NTH + i) * 128 + 128]
                if c % 2 == 0:
                    ev = S.op("vector", lambda v, dst=dst, bank=bank: v.tensor_copy(
                        dst, ps_t[:, bank * 1024:bank * 1024 + 128]), [tr])
                else:
                    ev = S.op("scalar", lambda s_, dst=dst, bank=bank: s_.copy(
                        dst, ps_t[:, bank * 1024:bank * 1024 + 128]), [tr])
                tp_banklast[bank] = ev
                evs.append(ev)
            xn_war = evs[-1]
            last_evs = evs
        xnT_all = [last_evs[-2], last_evs[-1]]

        wload = [None, None]
        pa_banks = PsumBanks(4)
        q_last = k_last = v_last = None
        bankrot = 0
        for s in range(24):
            bi = s % 2
            ld = S.dma("gpsimd", "lw%d" % bi,
                       lambda e, s=s, bi=bi: e.dma_start(
                           out=wsl[bi].rearrange("p (c e) -> p c e", e=256),
                           in_=waT.rearrange("(c p) e -> p c e", p=128)[:, :, s * 256:(s + 1) * 256]),
                       deps=[wload[bi]])
            kind = "q" if s < 8 else ("k" if s < 16 else "v")
            lastmm = None
            if kind in ("q", "k"):
                ntok = TPC if kind == "q" else NTH * 128
                tile0 = 1 if kind == "q" else 0
                dstbuf = qf if kind == "q" else kf
                npass = (ntok + 511) // 512
                for ec in range(2):
                    e_glob = (s % 8) * 2 + ec
                    for tp in range(npass):
                        t0 = tp * 512
                        tw = min(512, ntok - t0)
                        bank = bankrot % 4; bankrot += 1
                        mm = None
                        for dc in range(DCH):
                            src_t0 = (tile0 * 128) + t0
                            mm = S.op("tensor", lambda t, dc=dc, ec=ec, bank=bank, bi=bi, tw=tw, src_t0=src_t0: t.matmul(
                                ps_a[:, bank * 512:bank * 512 + tw],
                                wsl[bi][:, dc * 256 + ec * 128: dc * 256 + ec * 128 + 128],
                                bigA[:, dc * (NTH * 128) + src_t0: dc * (NTH * 128) + src_t0 + tw],
                                start=(dc == 0), stop=(dc == DCH - 1)),
                                ([ld, pa_banks.last[bank]] + xnT_all if dc == 0 else []))
                        dst = dstbuf[:, e_glob * ntok + t0: e_glob * ntok + t0 + tw]
                        scale = (1.0 / 128.0) if kind == "q" else 1.0
                        ev = S.op("scalar", lambda s_, dst=dst, bank=bank, tw=tw, scale=scale: s_.activation(
                            out=dst, in_=ps_a[:, bank * 512:bank * 512 + tw],
                            func=AF.Identity, scale=scale), [mm])
                        pa_banks.use(bank, ev)
                        lastmm = ev
                        if kind == "q":
                            q_last = ev
                        else:
                            k_last = ev
            else:
                e0 = (s - 16) * 256
                for i in range(NTH):
                    bank = bankrot % 4; bankrot += 1
                    mm = None
                    for dc in range(DCH):
                        mm = S.op("tensor", lambda t, dc=dc, i=i, bank=bank, bi=bi: t.matmul(
                            ps_a[:, bank * 512:bank * 512 + 256],
                            bigA[:, dc * (NTH * 128) + i * 128: dc * (NTH * 128) + (i + 1) * 128],
                            wsl[bi][:, dc * 256: (dc + 1) * 256],
                            start=(dc == 0), stop=(dc == DCH - 1)),
                            ([ld, pa_banks.last[bank]] + xnT_all if dc == 0 else []))
                    dst = vt[:, i * D + e0: i * D + e0 + 256]
                    ev = S.op("scalar", lambda s_, dst=dst, bank=bank: s_.activation(
                        out=dst, in_=ps_a[:, bank * 512:bank * 512 + 256],
                        func=AF.Identity), [mm])
                    pa_banks.use(bank, ev)
                    lastmm = ev
                    v_last = ev
            wload[bi] = lastmm

        proj_all = [q_last, k_last, v_last]
        ps_s_last = [None, None]
        att_war = None
        sev_war = None
        ofm_done = []
        for blk in range(NT):
            i = blk + 1
            evs = []
            for jj, jt in enumerate([i - 1, i]):
                mm = None
                for ec in range(DCH):
                    mm = S.op("tensor", lambda t, ec=ec, jj=jj, jt=jt, blk=blk: t.matmul(
                        ps_s[:, jj * 512:jj * 512 + 128],
                        kf[:, ec * (NTH * 128) + jt * 128: ec * (NTH * 128) + (jt + 1) * 128],
                        qf[:, ec * TPC + blk * 128: ec * TPC + (blk + 1) * 128],
                        start=(ec == 0), stop=(ec == DCH - 1)),
                        (([ps_s_last[jj]] if ps_s_last[jj] else proj_all) if ec == 0 else []))
                ev = S.op("scalar", lambda s_, jj=jj: s_.copy(
                    sev[:, jj * 128:(jj + 1) * 128], ps_s[:, jj * 512:jj * 512 + 128]),
                    [mm, sev_war])
                evs.append(ev)
            m1 = S.op("gpsimd", lambda g: g.affine_select(
                out=sev[:, 0:128], in_=sev[:, 0:128], pattern=[[-1, 128]],
                compare_op=ALU.is_ge, fill=0.0, base=0, channel_multiplier=1), [evs[0]])
            m2 = S.op("gpsimd", lambda g: g.affine_select(
                out=sev[:, 128:256], in_=sev[:, 128:256], pattern=[[1, 128]],
                compare_op=ALU.is_ge, fill=0.0, base=0, channel_multiplier=-1), [evs[1]])
            r1 = S.op("vector", lambda v: v.tensor_scalar(
                out=sev[:, :], in0=sev[:, :], scalar1=0.0, scalar2=None,
                op0=ALU.max), [m1, m2])
            r2 = S.op("vector", lambda v: v.tensor_tensor(
                out=att[:, :], in0=sev[:, :], in1=sev[:, :], op=ALU.mult),
                [r1, att_war])
            sev_war = r2
            ps_s_last = [evs[0], evs[1]]
            last_mm2 = None
            for ec in range(DCH):
                bank = ec % 4
                mm1 = S.op("tensor", lambda t, ec=ec, i=i, bank=bank: t.matmul(
                    ps_a[:, bank * 512:bank * 512 + 128],
                    vt[:, (i - 1) * D + ec * 128: (i - 1) * D + (ec + 1) * 128],
                    att[:, 0:128],
                    start=True, stop=False), [r2, pa_banks.last[bank]])
                mm2 = S.op("tensor", lambda t, ec=ec, i=i, bank=bank: t.matmul(
                    ps_a[:, bank * 512:bank * 512 + 128],
                    vt[:, i * D + ec * 128: i * D + (ec + 1) * 128],
                    att[:, 128:256],
                    start=False, stop=True), [])
                last_mm2 = mm2
                dst = bigA[:, ec * TPC + blk * 128: ec * TPC + (blk + 1) * 128]
                if ec % 2 == 0:
                    ev = S.op("vector", lambda v, dst=dst, bank=bank: v.tensor_copy(
                        dst, ps_a[:, bank * 512:bank * 512 + 128]), [mm2])
                else:
                    ev = S.op("scalar", lambda s_, dst=dst, bank=bank: s_.copy(
                        dst, ps_a[:, bank * 512:bank * 512 + 128]), [mm2])
                pa_banks.use(bank, ev)
                if ec % 2 == 0:
                    ofm_last_v = ev
                else:
                    ofm_last_s = ev
                ofm_done.append(ev)
            att_war = last_mm2

        wslabs4 = [kf[:, 0:8192], kf[:, 8192:16384], vt[:, 0:8192], vt[:, 8192:16384]]
        wld4 = []
        for s4 in range(4):
            ld = S.dma("gpsimd", "lo%d" % (s4 % 2),
                       lambda e, s4=s4: e.dma_start(
                           out=wslabs4[s4].rearrange("p (c e) -> p c e", e=512),
                           in_=woT.rearrange("(c p) e -> p c e", p=128)[:, :, s4 * 512:(s4 + 1) * 512]),
                       deps=[ofm_last_v, ofm_last_s])
            wld4.append(ld)
        xr_war = None
        yst = [None, None]
        for blk in range(NT):
            xrl = S.dma("sync", "lr%d" % (blk % 2),
                        lambda e, blk=blk: e.dma_start(
                            out=xr[:], in_=xh[(blk + 1) * 128:(blk + 2) * 128, :]),
                        deps=[xr_war])
            adds = []
            for s4 in range(4):
                bank = s4 % 4
                yb = s4 % 2
                mm = None
                for ec in range(DCH):
                    mm = S.op("tensor", lambda t, ec=ec, blk=blk, bank=bank, s4=s4: t.matmul(
                        ps_a[:, bank * 512:bank * 512 + 512],
                        bigA[:, ec * TPC + blk * 128: ec * TPC + (blk + 1) * 128],
                        wslabs4[s4][:, ec * 512:(ec + 1) * 512],
                        start=(ec == 0), stop=(ec == DCH - 1)),
                        ([wld4[s4], pa_banks.last[bank], ofm_last_v, ofm_last_s] if ec == 0 else []))
                add = S.op("vector", lambda v, s4=s4, bank=bank, yb=yb: v.tensor_tensor(
                    out=ybuf[yb][:, :],
                    in0=ps_a[:, bank * 512:bank * 512 + 512],
                    in1=xr[:, s4 * 512:(s4 + 1) * 512], op=ALU.add),
                    [mm, xrl, yst[yb]])
                pa_banks.use(bank, add)
                adds.append(add)
                yst[yb] = S.dma("sync", "sy%d" % yb,
                                lambda e, blk=blk, s4=s4, yb=yb: e.dma_start(
                                    out=y[blk * 128:(blk + 1) * 128, s4 * 512:(s4 + 1) * 512],
                                    in_=ybuf[yb][:, :]),
                                deps=[add])
            xr_war = adds[-1]
        S.wait("sync", [yst[0], yst[1]])
        S.finish(stack)
    return nc



def build_ffn0():
    nc = bass.Bass("TRN2", target_bir_lowering=False)
    x = nc.dram_tensor("x", [TPC, D], F32, kind="ExternalInput").ap()
    w1qT = nc.dram_tensor("w1qT", [D, DFF], F32, kind="ExternalInput").ap()
    b1T = nc.dram_tensor("b1T", [128, DFF // 128], F32, kind="ExternalInput").ap()
    w2qT = nc.dram_tensor("w2qT", [DFF, D], F32, kind="ExternalInput").ap()
    b2T = nc.dram_tensor("b2T", [128, DCH], F32, kind="ExternalInput").ap()
    y = nc.dram_tensor("y", [TPC, D], F32, kind="ExternalOutput").ap()
    FCH = DFF // 128

    S = Sched(nc)
    with contextlib.ExitStack() as stack:
        sb = lambda n, sh, dt=F32: stack.enter_context(nc.sbuf_tensor(n, sh, dt))
        xt = [sb("xt%d" % i, [128, D]) for i in range(4)]
        xn = sb("xn", [128, D], BF16)
        st6 = sb("st6", [128, 24]); st2 = sb("st2", [128, 2])
        rstd = sb("rstd", [128, 1]); mbias = sb("mbias", [128, 1])
        ones = sb("ones", [128, 128], BF16)
        ident = sb("ident", [128, 128], BF16)
        xnT = sb("xnT", [128, DCH * 512], BF16)
        hsb = sb("hsb", [128, FCH * 512], BF16)
        wsl = [sb("wsl%d" % i, [128, DCH * 256], BF16) for i in range(2)]
        ws2 = [sb("ws2%d" % i, [128, FCH * 128], BF16) for i in range(2)]
        b1s = sb("b1s", [128, DFF // 128])
        b2s = sb("b2s", [128, DCH])
        yf = sb("yf", [128, 512])
        ps_t = stack.enter_context(nc.psum_tensor("ps_t", [128, 2048], BF16))
        ps_a = stack.enter_context(nc.psum_tensor("ps_a", [128, 2048], F32))

        onesf = sb("onesf", [128, 128])
        identf = sb("identf", [128, 128])
        idt = emit_identity(S, ones[:], ident[:])
        idtf = emit_identity(S, onesf[:], identf[:])
        lb1 = S.dma("sync", "lb", lambda e: e.dma_start(out=b1s[:], in_=b1T[:, :]))
        lb2 = S.dma("sync", "lb", lambda e: e.dma_start(out=b2s[:], in_=b2T[:, :]))

        pa = PsumBanks(4)
        tp_last = [None, None]
        w1_war = [None, None]
        w2_war = [None, None]
        xt_war = [None] * 4
        yf_war = None
        for half in range(2):
            th0 = half * 512
            xn_war = None
            lnlast = None
            evs = None
            for it in range(4):
                ld = S.dma("sync", "lx%d" % (it % 2),
                           lambda e, it=it, th0=th0: e.dma_start(
                               out=xt[it][:], in_=x[th0 + it * 128: th0 + (it + 1) * 128, :]),
                           deps=[xt_war[it]])
                lnt, _ = emit_ln(S, xt[it][:], xn[:], st6[:], st2[:], rstd[:], mbias[:],
                                 [ld, lnlast, xn_war])
                lnlast = lnt
                evs = []
                for c in range(DCH):
                    bank = c % 2
                    tr = S.op("tensor", lambda t, c=c, bank=bank: t.transpose(
                        ps_t[:, bank * 1024:bank * 1024 + 128], xn[:, c * 128:(c + 1) * 128],
                        ident[:]), [lnt, idt, tp_last[bank]])
                    dst = xnT[:, c * 512 + it * 128: c * 512 + (it + 1) * 128]
                    if c % 2 == 0:
                        ev = S.op("vector", lambda v, dst=dst, bank=bank: v.tensor_copy(
                            dst, ps_t[:, bank * 1024:bank * 1024 + 128]), [tr])
                    else:
                        ev = S.op("scalar", lambda s_, dst=dst, bank=bank: s_.copy(
                            dst, ps_t[:, bank * 1024:bank * 1024 + 128]), [tr])
                    tp_last[bank] = ev
                    evs.append(ev)
                xn_war = evs[-1]
            xnT_all = [evs[-2], evs[-1]]

            h_done_v = h_done_s = None
            for s in range(32):
                bi = s % 2
                ld = S.dma("gpsimd", "lw%d" % bi,
                           lambda e, s=s, bi=bi: e.dma_start(
                               out=wsl[bi].rearrange("p (c e) -> p c e", e=256),
                               in_=w1qT.rearrange("(c p) e -> p c e", p=128)[:, :, s * 256:(s + 1) * 256]),
                           deps=[w1_war[bi]])
                lastev = None
                for fc2 in range(2):
                    fc = s * 2 + fc2
                    bank = fc % 4
                    mm = None
                    for dc in range(DCH):
                        mm = S.op("tensor", lambda t, dc=dc, fc2=fc2, bank=bank, bi=bi: t.matmul(
                            ps_a[:, bank * 512:bank * 512 + 512],
                            wsl[bi][:, dc * 256 + fc2 * 128: dc * 256 + fc2 * 128 + 128],
                            xnT[:, dc * 512:(dc + 1) * 512],
                            start=(dc == 0), stop=(dc == DCH - 1)),
                            ([ld, pa.last[bank]] + xnT_all if dc == 0 else []))
                    dst = hsb[:, fc * 512:(fc + 1) * 512]
                    ev = S.op("scalar", lambda s_, dst=dst, bank=bank, fc=fc: s_.activation(
                        out=dst, in_=ps_a[:, bank * 512:bank * 512 + 512],
                        func=AF.Gelu, bias=b1s[:, fc:fc + 1]), [mm, lb1])
                    pa.use(bank, ev)
                    lastev = ev
                h_done_s = lastev
                w1_war[bi] = lastev

            for s in range(DCH):
                bi = s % 2
                lds = []
                for g in range(4):
                    lds.append(S.dma("gpsimd", "l2%d" % bi,
                               lambda e, s=s, bi=bi, g=g: e.dma_start(
                                   out=ws2[bi].rearrange("p (c e) -> p c e", e=128)[:, g * 16:(g + 1) * 16, :],
                                   in_=w2qT.rearrange("(c p) e -> p c e", p=128)[:, g * 16:(g + 1) * 16, s * 128:(s + 1) * 128]),
                               deps=[w2_war[bi]]))
                bank = s % 2
                mm = None
                for fc in range(FCH):
                    mm = S.op("tensor", lambda t, fc=fc, bank=bank, bi=bi: t.matmul(
                        ps_a[:, bank * 512:bank * 512 + 512],
                        ws2[bi][:, fc * 128:(fc + 1) * 128],
                        hsb[:, fc * 512:(fc + 1) * 512],
                        start=(fc == 0), stop=(fc == FCH - 1)),
                        ([h_done_s, pa.last[bank]] + lds if fc == 0 else []))
                w2_war[bi] = mm
                ev = S.op("scalar", lambda s_, bank=bank, s=s: s_.activation(
                    out=yf[:, :], in_=ps_a[:, bank * 512:bank * 512 + 512],
                    func=AF.Identity, bias=b2s[:, s:s + 1]), [mm, lb2, yf_war])
                pa.use(bank, ev)
                for it in range(4):
                    tb = 2 + (it % 2)
                    tr = S.op("tensor", lambda t, it=it, tb=tb: t.transpose(
                        ps_a[:, tb * 512:tb * 512 + 128], yf[:, it * 128:(it + 1) * 128],
                        identf[:]), [ev, pa.last[tb], idtf])
                    add = S.op("vector", lambda v, it=it, tb=tb, s=s: v.tensor_tensor(
                        out=xt[it][:, s * 128:(s + 1) * 128],
                        in0=ps_a[:, tb * 512:tb * 512 + 128],
                        in1=xt[it][:, s * 128:(s + 1) * 128], op=ALU.add), [tr])
                    pa.use(tb, add)
                    xt_war[it] = add
                    yf_war = tr
            sts = []
            for it in range(4):
                st = S.dma("sync", "sy%d" % (it % 2),
                           lambda e, it=it, th0=th0: e.dma_start(
                               out=y[th0 + it * 128: th0 + (it + 1) * 128, :], in_=xt[it][:]),
                           deps=[xt_war[it]])
                xt_war[it] = st
                sts.append(st)
            yst_last = sts[-1]
        S.wait("sync", sts)
        S.finish(stack)
    return nc



def build_attnb_qkv(binarize=True, qscale=1.0 / float(np.sqrt(128.0))):
    nc = bass.Bass("TRN2", target_bir_lowering=False)
    x = nc.dram_tensor("x", [TPC, D], F32, kind="ExternalInput").ap()
    wqT = nc.dram_tensor("wqT", [D, D], F32, kind="ExternalInput").ap()
    wkT = nc.dram_tensor("wkT", [D, D], F32, kind="ExternalInput").ap()
    wvT = nc.dram_tensor("wvT", [D, D], F32, kind="ExternalInput").ap()
    qs = nc.dram_tensor("qs", [TPC, D], F32, kind="ExternalOutput").ap()
    ks = nc.dram_tensor("ks", [TPC, D], F32, kind="ExternalOutput").ap()
    vs = nc.dram_tensor("vs", [TPC, D], F32, kind="ExternalOutput").ap()

    S = Sched(nc)
    with contextlib.ExitStack() as stack:
        sb = lambda n, sh, dt=F32: stack.enter_context(nc.sbuf_tensor(n, sh, dt))
        xt = sb("xt", [128, D])
        xn = sb("xn", [128, D], BF16)
        st6 = sb("st6", [128, 24]); st2 = sb("st2", [128, 2])
        rstd = sb("rstd", [128, 1]); mbias = sb("mbias", [128, 1])
        ones = sb("ones", [128, 128], BF16)
        ident = sb("ident", [128, 128], BF16)
        xnT = sb("xnT", [128, DCH * TPC], BF16)
        wsl = [sb("wsl%d" % i, [128, DCH * 512], BF16) for i in range(2)]
        stg = [sb("stg%d" % i, [128, 512]) for i in range(4)]
        ps_t = stack.enter_context(nc.psum_tensor("ps_t", [128, 2048], BF16))
        ps_a = stack.enter_context(nc.psum_tensor("ps_a", [128, 2048], F32))

        idt = emit_identity(S, ones[:], ident[:])
        tp_last = [None, None]
        lnlast = None
        xn_war = None
        xt_war = None
        evs = None
        for i in range(NT):
            ld = S.dma("sync", "lx%d" % (i % 2),
                       lambda e, i=i: e.dma_start(out=xt[:], in_=x[i * 128:(i + 1) * 128, :]),
                       deps=[xt_war])
            lnt, _ = emit_ln(S, xt[:], xn[:], st6[:], st2[:], rstd[:], mbias[:],
                             [ld, lnlast, xn_war])
            xt_war = lnt
            lnlast = lnt
            evs = []
            for c in range(DCH):
                bank = c % 2
                tr = S.op("tensor", lambda t, c=c, bank=bank: t.transpose(
                    ps_t[:, bank * 1024:bank * 1024 + 128], xn[:, c * 128:(c + 1) * 128],
                    ident[:]), [lnt, idt, tp_last[bank]])
                dst = xnT[:, c * TPC + i * 128: c * TPC + (i + 1) * 128]
                if c % 2 == 0:
                    ev = S.op("vector", lambda v, dst=dst, bank=bank: v.tensor_copy(
                        dst, ps_t[:, bank * 1024:bank * 1024 + 128]), [tr])
                else:
                    ev = S.op("scalar", lambda s_, dst=dst, bank=bank: s_.copy(
                        dst, ps_t[:, bank * 1024:bank * 1024 + 128]), [tr])
                tp_last[bank] = ev
                evs.append(ev)
            xn_war = evs[-1]
        xnT_all = [evs[-2], evs[-1]]

        SCALE = qscale
        pa = PsumBanks(4)
        w_war = [None, None]
        stg_war = [None] * 4
        sts = []
        si = 0
        kv_kind = "kv" if binarize else "plain"
        for wi, (wT, out, kind) in enumerate([(wqT, qs, "q"), (wkT, ks, kv_kind), (wvT, vs, kv_kind)]):
            for s in range(4):
                bi = (wi * 4 + s) % 2
                ld = S.dma("gpsimd", "lw%d" % bi,
                           lambda e, wT=wT, s=s, bi=bi: e.dma_start(
                               out=wsl[bi].rearrange("p (c e) -> p c e", e=512),
                               in_=wT.rearrange("(c p) e -> p c e", p=128)[:, :, s * 512:(s + 1) * 512]),
                           deps=[w_war[bi]])
                lastop = None
                for it in range(NT):
                    bank = (it % 4)
                    mm = None
                    for dc in range(DCH):
                        mm = S.op("tensor", lambda t, dc=dc, it=it, bank=bank, bi=bi: t.matmul(
                            ps_a[:, bank * 512:bank * 512 + 512],
                            xnT[:, dc * TPC + it * 128: dc * TPC + (it + 1) * 128],
                            wsl[bi][:, dc * 512:(dc + 1) * 512],
                            start=(dc == 0), stop=(dc == DCH - 1)),
                            ([ld, pa.last[bank]] + xnT_all if dc == 0 else []))
                    sg = si % 4
                    si += 1
                    if kind == "q" or kind == "plain":
                        scl_ = SCALE if kind == "q" else 1.0
                        ev = S.op("scalar", lambda s_, sg=sg, bank=bank, scl_=scl_: s_.activation(
                            out=stg[sg][:], in_=ps_a[:, bank * 512:bank * 512 + 512],
                            func=AF.Identity, scale=scl_), [mm, stg_war[sg]])
                    else:
                        e1 = S.op("vector", lambda v, sg=sg, bank=bank: v.tensor_scalar(
                            out=stg[sg][:], in0=ps_a[:, bank * 512:bank * 512 + 512],
                            scalar1=0.0, scalar2=2.0, op0=ALU.is_ge, op1=ALU.mult),
                            [mm, stg_war[sg]])
                        ev = S.op("vector", lambda v, sg=sg: v.tensor_scalar(
                            out=stg[sg][:], in0=stg[sg][:], scalar1=1.0, scalar2=None,
                            op0=ALU.subtract), [e1])
                    pa.use(bank, ev)
                    st = S.dma("sync", "so%d" % (si % 2),
                               lambda e, out=out, it=it, s=s, sg=sg: e.dma_start(
                                   out=out[it * 128:(it + 1) * 128, s * 512:(s + 1) * 512],
                                   in_=stg[sg][:]), deps=[ev])
                    stg_war[sg] = st
                    sts.append(st)
                    lastop = ev
                w_war[bi] = lastop
        S.wait("sync", sts[-2:])
        S.finish(stack)
    return nc



def build_attnb_scan():
    nc = bass.Bass("TRN2", target_bir_lowering=False)
    NC_ = T // 128
    ins = {}
    outs = {}
    for hh in range(4):
        ins["qf%d" % hh] = nc.dram_tensor("qf%d" % hh, [128, T], F32, kind="ExternalInput").ap()
        ins["kf%d" % hh] = nc.dram_tensor("kf%d" % hh, [128, T], F32, kind="ExternalInput").ap()
        ins["kt%d" % hh] = nc.dram_tensor("kt%d" % hh, [T, 128], F32, kind="ExternalInput").ap()
        ins["vt%d" % hh] = nc.dram_tensor("vt%d" % hh, [T, 128], F32, kind="ExternalInput").ap()
        outs[hh] = nc.dram_tensor("o%d" % hh, [T, 128], F32, kind="ExternalOutput").ap()

    S = Sched(nc)
    with contextlib.ExitStack() as stack:
        sb = lambda n, sh, dt=F32: stack.enter_context(nc.sbuf_tensor(n, sh, dt))
        qf = [sb("qfs%d" % h, [128, T], BF16) for h in range(4)]
        kf = [sb("kfs%d" % h, [128, T], BF16) for h in range(4)]
        kt = [sb("kts%d" % h, [128, T], BF16) for h in range(4)]
        vt = [sb("vts%d" % h, [128, T], BF16) for h in range(4)]
        ssb = sb("ssb", [128, 128])
        stb = sb("stb", [128, 128], BF16)
        state_acc = sb("state_acc", [128, 128])
        stateb = sb("stateb", [128, 128], BF16)
        osb = [sb("osb%d" % i, [128, 128]) for i in range(2)]
        ps = stack.enter_context(nc.psum_tensor("ps", [128, 1536], F32))

        lds = []
        for hh in range(4):
            for nm, dstb in (("qf", qf), ("kf", kf)):
                lds.append(S.dma("gpsimd", "l%d" % (len(lds) % 3),
                                 lambda e, nm=nm, hh=hh, dstb=dstb: e.dma_start(
                                     out=dstb[hh][:], in_=ins[nm + str(hh)][:, :]), deps=[]))
            for nm, dstb in (("kt", kt), ("vt", vt)):
                lds.append(S.dma("gpsimd", "l%d" % (len(lds) % 3),
                                 lambda e, nm=nm, hh=hh, dstb=dstb: e.dma_start(
                                     out=dstb[hh].rearrange("p (c e) -> p c e", e=128),
                                     in_=ins[nm + str(hh)].rearrange("(c p) e -> p c e", p=128)),
                                 deps=[]))
        ld_all = lds[-3:]

        st_done = [None, None]
        p_upd = None
        p_uadd = None
        p_stateb = None
        p_sTev = None
        p_cst = None
        p_intra = None
        p_oev = None
        p_cross = None
        for hh in range(4):
            zst = S.op("vector", lambda v: v.memset(state_acc[:], 0.0),
                       [p_stateb, p_uadd])
            p_uadd = zst
            for c in range(NC_):
                c0 = c * 128
                if c > 0:
                    p_stateb = S.op("vector", lambda v: v.tensor_copy(
                        stateb[:], state_acc[:]), [p_uadd, p_cross])
                smm = S.op("tensor", lambda t, hh=hh, c0=c0: t.matmul(
                    ps[:, 1024:1152], kf[hh][:, c0:c0 + 128], qf[hh][:, c0:c0 + 128],
                    start=True, stop=True), ld_all + [p_sTev])
                p_sTev = S.op("scalar", lambda s_: s_.copy(ssb[:], ps[:, 1024:1152]),
                              [smm, p_cst])
                msk = S.op("gpsimd", lambda g: g.affine_select(
                    out=ssb[:], in_=ssb[:], pattern=[[1, 128]], compare_op=ALU.is_ge,
                    fill=0.0, base=0, channel_multiplier=-1), [p_sTev])
                p_cst = S.op("vector", lambda v: v.tensor_copy(stb[:], ssb[:]),
                             [msk, p_intra])
                if c > 0:
                    p_cross = S.op("tensor", lambda t, hh=hh, c0=c0: t.matmul(
                        ps[:, 512:640], qf[hh][:, c0:c0 + 128], stateb[:],
                        start=True, stop=False), [p_stateb, p_oev])
                    p_intra = S.op("tensor", lambda t, hh=hh, c0=c0: t.matmul(
                        ps[:, 512:640], stb[:], vt[hh][:, c0:c0 + 128],
                        start=False, stop=True), [p_cst])
                else:
                    p_cross = None
                    p_intra = S.op("tensor", lambda t, hh=hh, c0=c0: t.matmul(
                        ps[:, 512:640], stb[:], vt[hh][:, c0:c0 + 128],
                        start=True, stop=True), [p_cst, p_oev])
                ob = c % 2
                p_oev = S.op("scalar", lambda s_, ob=ob: s_.copy(
                    osb[ob][:], ps[:, 512:640]), [p_intra, st_done[ob]])
                st_done[ob] = S.dma("sync", "st%d" % ob,
                                    lambda e, hh=hh, c0=c0, ob=ob: e.dma_start(
                                        out=outs[hh][c0:c0 + 128, :], in_=osb[ob][:]),
                                    deps=[p_oev])
                if c < NC_ - 1:
                    p_upd = S.op("tensor", lambda t, hh=hh, c0=c0: t.matmul(
                        ps[:, 0:128], kt[hh][:, c0:c0 + 128], vt[hh][:, c0:c0 + 128],
                        start=True, stop=True), [p_uadd])
                    p_uadd = S.op("vector", lambda v: v.tensor_tensor(
                        out=state_acc[:], in0=state_acc[:], in1=ps[:, 0:128],
                        op=ALU.add), [p_upd, p_stateb])
        S.wait("sync", [st_done[0], st_done[1]])
        S.finish(stack)
    return nc



def build_proj_residual():
    nc = bass.Bass("TRN2", target_bir_lowering=False)
    x = nc.dram_tensor("x", [TPC, D], F32, kind="ExternalInput").ap()
    aT = nc.dram_tensor("aT", [D, TPC], F32, kind="ExternalInput").ap()
    m = nc.dram_tensor("m", [D, D], F32, kind="ExternalInput").ap()
    y = nc.dram_tensor("y", [TPC, D], F32, kind="ExternalOutput").ap()

    S = Sched(nc)
    with contextlib.ExitStack() as stack:
        sb = lambda n, sh, dt=F32: stack.enter_context(nc.sbuf_tensor(n, sh, dt))
        ab = sb("ab", [128, DCH * TPC], BF16)
        wsl = [sb("wsl%d" % i, [128, DCH * 512], BF16) for i in range(4)]
        xr = sb("xr", [128, D])
        yb = [sb("yb%d" % i, [128, 512]) for i in range(2)]
        ps_a = stack.enter_context(nc.psum_tensor("ps_a", [128, 2048], F32))

        la = S.dma("gpsimd", "la", lambda e: e.dma_start(
            out=ab.rearrange("p (c e) -> p c e", e=TPC),
            in_=aT.rearrange("(c p) e -> p c e", p=128)))
        pa = PsumBanks(4)
        w_war = [None, None]
        xr_war = None
        yst = [None, None]
        wld = []
        for s in range(4):
            ld = S.dma("gpsimd", "lw%d" % (s % 2),
                       lambda e, s=s: e.dma_start(
                           out=wsl[s].rearrange("p (c e) -> p c e", e=512),
                           in_=m.rearrange("(c p) e -> p c e", p=128)[:, :, s * 512:(s + 1) * 512]))
            wld.append(ld)
        for it in range(NT):
            xrl = S.dma("sync", "lr%d" % (it % 2),
                        lambda e, it=it: e.dma_start(
                            out=xr[:], in_=x[it * 128:(it + 1) * 128, :]), deps=[xr_war])
            adds = []
            for s in range(4):
                bank = s % 4
                mm = None
                for dc in range(DCH):
                    mm = S.op("tensor", lambda t, dc=dc, it=it, bank=bank, s=s: t.matmul(
                        ps_a[:, bank * 512:bank * 512 + 512],
                        ab[:, dc * TPC + it * 128: dc * TPC + (it + 1) * 128],
                        wsl[s][:, dc * 512:(dc + 1) * 512],
                        start=(dc == 0), stop=(dc == DCH - 1)),
                        ([wld[s], la, pa.last[bank]] if dc == 0 else []))
                yb_i = s % 2
                add = S.op("vector", lambda v, s=s, bank=bank, yb_i=yb_i: v.tensor_tensor(
                    out=yb[yb_i][:], in0=ps_a[:, bank * 512:bank * 512 + 512],
                    in1=xr[:, s * 512:(s + 1) * 512], op=ALU.add),
                    [mm, xrl, yst[yb_i]])
                pa.use(bank, add)
                adds.append(add)
                yst[yb_i] = S.dma("sync", "sy%d" % yb_i,
                                  lambda e, it=it, s=s, yb_i=yb_i: e.dma_start(
                                      out=y[it * 128:(it + 1) * 128, s * 512:(s + 1) * 512],
                                      in_=yb[yb_i][:]), deps=[add])
            xr_war = adds[-1]
        S.wait("sync", [yst[0], yst[1]])
        S.finish(stack)
    return nc



def build_attna_causal():
    nc = bass.Bass("TRN2", target_bir_lowering=False)
    NQ = T // 128
    ins = {}
    outs = {}
    for hh in range(4):
        ins["qf%d" % hh] = nc.dram_tensor("qf%d" % hh, [128, T], F32, kind="ExternalInput").ap()
        ins["kf%d" % hh] = nc.dram_tensor("kf%d" % hh, [128, T], F32, kind="ExternalInput").ap()
        ins["vt%d" % hh] = nc.dram_tensor("vt%d" % hh, [T, 128], F32, kind="ExternalInput").ap()
        outs[hh] = nc.dram_tensor("o%d" % hh, [T, 128], F32, kind="ExternalOutput").ap()

    S = Sched(nc)
    with contextlib.ExitStack() as stack:
        sb = lambda n, sh, dt=F32: stack.enter_context(nc.sbuf_tensor(n, sh, dt))
        qf = [sb("qfs%d" % h, [128, T], BF16) for h in range(4)]
        kf = [sb("kfs%d" % h, [128, T], BF16) for h in range(4)]
        vt = [sb("vts%d" % h, [128, T], BF16) for h in range(4)]
        ssb = sb("ssb", [128, T])
        pb = sb("pb", [128, T], BF16)
        ptb = [sb("ptb%d" % i, [128, 128], BF16) for i in range(2)]
        mx = sb("mx", [128, 1]); nmx = sb("nmx", [128, 1])
        sm = sb("sm", [128, 1]); rs = sb("rs", [128, 1])
        osb = [sb("osb%d" % i, [128, 128]) for i in range(2)]
        ones = sb("ones", [128, 128], BF16)
        ident = sb("ident", [128, 128], BF16)
        ps = stack.enter_context(nc.psum_tensor("ps", [128, 2048], F32))
        ps_t = stack.enter_context(nc.psum_tensor("ps_t", [128, 2048], BF16))

        cmask = sb("cmask", [128, 128])
        idt = emit_identity(S, ones[:], ident[:])
        cm0 = S.op("vector", lambda v: v.memset(cmask[:], 0.0))
        cm1 = S.op("gpsimd", lambda g: g.affine_select(
            out=cmask[:], in_=cmask[:], pattern=[[-1, 128]], compare_op=ALU.is_ge,
            fill=-1e38, base=0, channel_multiplier=1), [cm0])
        lds = []
        for hh in range(4):
            for nm, dstb in (("qf", qf), ("kf", kf)):
                lds.append(S.dma("gpsimd", "l%d" % (len(lds) % 3),
                                 lambda e, nm=nm, hh=hh, dstb=dstb: e.dma_start(
                                     out=dstb[hh][:], in_=ins[nm + str(hh)][:, :]), deps=[]))
            lds.append(S.dma("gpsimd", "l%d" % (len(lds) % 3),
                             lambda e, hh=hh: e.dma_start(
                                 out=vt[hh].rearrange("p (c e) -> p c e", e=128),
                                 in_=ins["vt%d" % hh].rearrange("(c p) e -> p c e", p=128)),
                             deps=[]))
        ld_all = lds[-3:]

        pa = PsumBanks(4)
        tp_last = [None, None]
        ssb_war = None
        pb_war = None
        st_done = [None, None]
        stat_war = None
        for hh in range(4):
            for qt in range(NQ):
                q0 = qt * 128
                nk = (qt + 1) * 128
                sevs = []
                for kb in range((nk + 511) // 512):
                    kw = min(512, nk - kb * 512)
                    bank = kb % 2
                    mm = S.op("tensor", lambda t, hh=hh, q0=q0, kb=kb, kw=kw, bank=bank: t.matmul(
                        ps[:, bank * 512:bank * 512 + kw],
                        qf[hh][:, q0:q0 + 128], kf[hh][:, kb * 512:kb * 512 + kw],
                        start=True, stop=True),
                        ld_all + [pa.last[bank]])
                    ev = S.op("scalar", lambda s_, kb=kb, kw=kw, bank=bank: s_.copy(
                        ssb[:, kb * 512:kb * 512 + kw], ps[:, bank * 512:bank * 512 + kw]),
                        [mm, ssb_war])
                    pa.use(bank, ev)
                    sevs.append(ev)
                msk = S.op("vector", lambda v, q0=q0: v.tensor_tensor(
                    out=ssb[:, q0:q0 + 128], in0=ssb[:, q0:q0 + 128], in1=cmask[:],
                    op=ALU.add), [sevs[-1], cm1])
                rmx = S.op("vector", lambda v, nk=nk: v.reduce_max(
                    out=mx[:], in_=ssb[:, 0:nk], axis=AX.X), sevs + [msk, stat_war])
                ngx = S.op("vector", lambda v: v.tensor_scalar(
                    out=nmx[:], in0=mx[:], scalar1=-1.0, scalar2=None, op0=ALU.mult), [rmx])
                ex = S.op("scalar", lambda s_, nk=nk: s_.activation(
                    out=pb[:, 0:nk], in_=ssb[:, 0:nk], func=AF.Exp, bias=nmx[:, 0:1]),
                    [ngx, pb_war])
                sumr = S.op("vector", lambda v, nk=nk: v.reduce_sum(
                    out=sm[:], in_=pb[:, 0:nk], axis=AX.X), [ex])
                rcp = S.op("vector", lambda v: v.reciprocal(out=rs[:], in_=sm[:]), [sumr])
                stat_war = rcp
                obank = 2 + (qt % 2)
                mm = None
                pb_readers = []
                for kt in range(qt + 1):
                    tb = kt % 2
                    tr = S.op("tensor", lambda t, kt=kt, tb=tb: t.transpose(
                        ps_t[:, tb * 1024:tb * 1024 + 128], pb[:, kt * 128:(kt + 1) * 128],
                        ident[:]), [ex, idt, tp_last[tb]])
                    pb_readers.append(tr)
                    pev = S.op("vector", lambda v, tb=tb: v.tensor_copy(
                        ptb[tb][:], ps_t[:, tb * 1024:tb * 1024 + 128]), [tr])
                    mm = S.op("tensor", lambda t, hh=hh, kt=kt, tb=tb, obank=obank, qt=qt: t.matmul(
                        ps[:, obank * 512:obank * 512 + 128],
                        ptb[tb][:], vt[hh][:, kt * 128:(kt + 1) * 128],
                        start=(kt == 0), stop=(kt == qt)),
                        [pev] + ([pa.last[obank]] if kt == 0 else []))
                    tp_last[tb] = mm
                pb_war = pb_readers[-1] if len(pb_readers) < 2 else pb_readers[-2]
                pb_war = pb_readers[-1]
                ob = qt % 2
                oev = S.op("scalar", lambda s_, ob=ob, obank=obank: s_.activation(
                    out=osb[ob][:], in_=ps[:, obank * 512:obank * 512 + 128],
                    func=AF.Identity, scale=rs[:, 0:1]), [mm, rcp, st_done[ob]])
                pa.use(obank, oev)
                st_done[ob] = S.dma("sync", "st%d" % ob,
                                    lambda e, hh=hh, q0=q0, ob=ob: e.dma_start(
                                        out=outs[hh][q0:q0 + 128, :], in_=osb[ob][:]),
                                    deps=[oev])
                ssb_war = ex
        S.wait("sync", [st_done[0], st_done[1]])
        S.finish(stack)
    return nc



def build_ffn1():
    nc = bass.Bass("TRN2", target_bir_lowering=False)
    x = nc.dram_tensor("x", [TPC, D], F32, kind="ExternalInput").ap()
    upqT = nc.dram_tensor("upqT", [D, DFF], F32, kind="ExternalInput").ap()
    dnqT = nc.dram_tensor("dnqT", [DFF, D], F32, kind="ExternalInput").ap()
    nw = nc.dram_tensor("nw", [1, D], F32, kind="ExternalInput").ap()
    gmT = nc.dram_tensor("gmT", [128, DCH], F32, kind="ExternalInput").ap()
    scl = nc.dram_tensor("scl", [1, 4], F32, kind="ExternalInput").ap()
    y = nc.dram_tensor("y", [TPC, D], F32, kind="ExternalOutput").ap()
    FCH = DFF // 128
    EPS_NORM = 1e-6

    S = Sched(nc)
    with contextlib.ExitStack() as stack:
        sb = lambda n, sh, dt=F32: stack.enter_context(nc.sbuf_tensor(n, sh, dt))
        xt = [sb("xt%d" % i, [128, D]) for i in range(4)]
        xn = sb("xn", [128, D])
        xn2 = sb("xn2", [128, D])
        xq = sb("xq", [128, D])
        xqb = sb("xqb", [128, D], BF16)
        st6 = sb("st6", [128, 24]); st2 = sb("st2", [128, 2])
        rstd = sb("rstd", [128, 1]); mbias = sb("mbias", [128, 1])
        ms2 = sb("ms2", [128, 1]); r2 = sb("r2", [128, 1])
        gam = sb("gam", [128, 1]); rq = sb("rq", [128, 1]); sx = sb("sx", [128, 1])
        mfl = sb("mfl", [128, 1])
        ones = sb("ones", [128, 128], BF16)
        ident = sb("ident", [128, 128], BF16)
        onesf = sb("onesf", [128, 128])
        identf = sb("identf", [128, 128])
        nws = sb("nws", [1, D])
        nwb = sb("nwb", [128, D])
        scs = sb("scs", [1, 4])
        sup = sb("sup", [128, 1])
        sdn = sb("sdn", [128, 1])
        gmsc = sb("gmsc", [128, DCH])
        xnT = sb("xnT", [128, DCH * 512], BF16)
        hsb = sb("hsb", [128, FCH * 512], BF16)
        wsl = [sb("wsl%d" % i, [128, DCH * 256], BF16) for i in range(2)]
        ws2 = [sb("ws2", [128, FCH * 128], BF16)]
        yf = sb("yf", [128, 512])
        ps_t = stack.enter_context(nc.psum_tensor("ps_t", [128, 2048], BF16))
        ps_a = stack.enter_context(nc.psum_tensor("ps_a", [128, 2048], F32))

        idt = emit_identity(S, ones[:], ident[:])
        idtf = emit_identity(S, onesf[:], identf[:])
        mft = emit_mflag(S, mfl[:])
        lnw = S.dma("sync", "lc", lambda e: e.dma_start(out=nws[:], in_=nw[:, :]))
        lgm = S.dma("sync", "lc", lambda e: e.dma_start(out=gmsc[:], in_=gmT[:, :]))
        lsc = S.dma("sync", "lc", lambda e: e.dma_start(out=scs[:], in_=scl[:, :]))
        bn_ms = []
        for i in range(4):
            bm = S.op("tensor", lambda t, i=i: t.matmul(
                ps_a[:, i * 512:(i + 1) * 512], onesf[0:1, :], nws[0:1, i * 512:(i + 1) * 512],
                start=True, stop=True), [lnw, idtf])
            ev = S.op("vector", lambda v, i=i: v.tensor_copy(
                nwb[:, i * 512:(i + 1) * 512], ps_a[:, i * 512:(i + 1) * 512]), [bm])
            bn_ms.append(ev)
        bm2 = S.op("tensor", lambda t: t.matmul(
            ps_a[:, 0:2], onesf[0:1, :], scs[0:1, 1:3], start=True, stop=True),
            [lsc, idtf, bn_ms[0]])
        sev = S.op("vector", lambda v: v.tensor_copy(sup[:], ps_a[:, 0:1]), [bm2])
        sev2 = S.op("vector", lambda v: v.tensor_copy(sdn[:], ps_a[:, 1:2]), [bm2])
        bm3 = S.op("tensor", lambda t: t.matmul(
            ps_a[:, 512:513], onesf[0:1, :], scs[0:1, 3:4], start=True, stop=True),
            [lsc, idtf, bn_ms[1]])
        sev3 = S.op("vector", lambda v: v.tensor_copy(sdn[:], ps_a[:, 512:513]), [bm3])
        gsc = S.op("vector", lambda v: v.tensor_scalar(
            out=gmsc[:], in0=gmsc[:], scalar1=sdn[:, 0:1], scalar2=None, op0=ALU.mult),
            [lgm, sev3])

        pa = PsumBanks(4)
        pa.use(0, sev); pa.use(1, sev3)
        for i, e_ in enumerate(bn_ms):
            pa.use(i, e_ if i > 1 else pa.last[i])
        tp_last = [None, None]
        w1_war = [None, None]
        w2_war = [None, None]
        xt_war = [None] * 4
        yf_war = None
        sts = []
        for half in range(2):
            th0 = half * 512
            xn_war = None
            lnlast = None
            evs = None
            for it in range(4):
                ld = S.dma("sync", "lx%d" % (it % 2),
                           lambda e, it=it, th0=th0: e.dma_start(
                               out=xt[it][:], in_=x[th0 + it * 128: th0 + (it + 1) * 128, :]),
                           deps=[xt_war[it]])
                lnt, _ = emit_ln(S, xt[it][:], xn[:], st6[:], st2[:], rstd[:], mbias[:],
                                 [ld, lnlast, xn_war])
                lnlast = lnt
                b1 = []
                for i4 in range(4):
                    b1.append(S.op("vector", lambda v, i4=i4: v.bn_stats(
                        st6[:, 6 * i4:6 * (i4 + 1)], xn[:, i4 * 512:(i4 + 1) * 512]), [lnt]))
                b2 = S.op("vector", lambda v: v.bn_aggr(st2[:], st6[:]), b1)
                m1 = S.op("vector", lambda v: v.tensor_tensor(
                    out=ms2[:], in0=st2[:, 0:1], in1=st2[:, 0:1], op=ALU.mult), [b2])
                m2 = S.op("vector", lambda v: v.tensor_tensor(
                    out=ms2[:], in0=ms2[:], in1=st2[:, 1:2], op=ALU.add), [m1])
                m3 = S.op("vector", lambda v: v.tensor_scalar(
                    out=r2[:], in0=ms2[:], scalar1=EPS_NORM, scalar2=None, op0=ALU.add), [m2])
                m4 = S.op("scalar", lambda s_: s_.activation(out=r2[:], in_=r2[:], func=AF.Sqrt), [m3])
                m5 = S.op("vector", lambda v: v.reciprocal(out=r2[:], in_=r2[:]), [m4])
                x2 = S.op("vector", lambda v: v.scalar_tensor_tensor(
                    out=xn2[:], in0=xn[:], scalar=r2[:, 0:1], in1=nwb[:],
                    op0=ALU.mult, op1=ALU.mult), [m5] + bn_ms)
                g1 = S.op("vector", lambda v: v.reduce_max(
                    out=gam[:], in_=xn2[:], axis=AX.X, apply_absolute_value=True), [x2])
                g2 = S.op("vector", lambda v: v.tensor_scalar(
                    out=gam[:], in0=gam[:], scalar1=1e-8, scalar2=None, op0=ALU.max), [g1])
                g3 = S.op("vector", lambda v: v.reciprocal(out=rq[:], in_=gam[:]), [g2])
                g4 = S.op("vector", lambda v: v.tensor_scalar(
                    out=rq[:], in0=rq[:], scalar1=127.0, scalar2=None, op0=ALU.mult), [g3])
                g5 = S.op("vector", lambda v: v.tensor_scalar(
                    out=sx[:], in0=gam[:], scalar1=1.0 / 127.0, scalar2=None, op0=ALU.mult), [g2])
                ra = S.op("vector", lambda v: v.tensor_scalar(
                    out=xq[:], in0=xn2[:], scalar1=rq[:, 0:1], scalar2=None, op0=ALU.mult),
                    [g4, x2])
                radd = S.op("vector", lambda v, it=it: v.tensor_tensor(
                    out=xt[it][:], in0=xt[it][:], in1=xn[:], op=ALU.add), [ra, lnt])
                xn_war = radd
                rnd = magic_round(S, xn2[:], xn[:], xq[:], mfl[:, 0:1], [ra, radd, mft])
                xb = S.op("vector", lambda v: v.tensor_scalar(
                    out=xqb[:], in0=xn2[:], scalar1=sx[:, 0:1], scalar2=None, op0=ALU.mult),
                    [rnd, g5])
                evs = []
                for c in range(DCH):
                    bank = c % 2
                    tr = S.op("tensor", lambda t, c=c, bank=bank: t.transpose(
                        ps_t[:, bank * 1024:bank * 1024 + 128], xqb[:, c * 128:(c + 1) * 128],
                        ident[:]), [xb, idt, tp_last[bank]])
                    dst = xnT[:, c * 512 + it * 128: c * 512 + (it + 1) * 128]
                    if c % 2 == 0:
                        ev = S.op("vector", lambda v, dst=dst, bank=bank: v.tensor_copy(
                            dst, ps_t[:, bank * 1024:bank * 1024 + 128]), [tr])
                    else:
                        ev = S.op("scalar", lambda s_, dst=dst, bank=bank: s_.copy(
                            dst, ps_t[:, bank * 1024:bank * 1024 + 128]), [tr])
                    tp_last[bank] = ev
                    evs.append(ev)
                lnlast = evs[-1]
            xnT_all = [evs[-2], evs[-1]]

            h_done_s = None
            for s in range(32):
                bi = s % 2
                ld = S.dma("gpsimd", "lw%d" % bi,
                           lambda e, s=s, bi=bi: e.dma_start(
                               out=wsl[bi].rearrange("p (c e) -> p c e", e=256),
                               in_=upqT.rearrange("(c p) e -> p c e", p=128)[:, :, s * 256:(s + 1) * 256]),
                           deps=[w1_war[bi]])
                lastev = None
                for fc2 in range(2):
                    fc = s * 2 + fc2
                    bank = fc % 4
                    mm = None
                    for dc in range(DCH):
                        mm = S.op("tensor", lambda t, dc=dc, fc2=fc2, bank=bank, bi=bi: t.matmul(
                            ps_a[:, bank * 512:bank * 512 + 512],
                            wsl[bi][:, dc * 256 + fc2 * 128: dc * 256 + fc2 * 128 + 128],
                            xnT[:, dc * 512:(dc + 1) * 512],
                            start=(dc == 0), stop=(dc == DCH - 1)),
                            ([ld, pa.last[bank]] + xnT_all if dc == 0 else []))
                    dst = hsb[:, fc * 512:(fc + 1) * 512]
                    ev = S.op("scalar", lambda s_, dst=dst, bank=bank: s_.activation(
                        out=dst, in_=ps_a[:, bank * 512:bank * 512 + 512],
                        func=AF.Silu, scale=sup[:, 0:1]), [mm, sev])
                    pa.use(bank, ev)
                    lastev = ev
                h_done_s = lastev
                w1_war[bi] = lastev

            for s in range(DCH):
                bi = 0
                lds2 = []
                for g in range(4):
                    lds2.append(S.dma("gpsimd", "l2%d" % bi,
                               lambda e, s=s, bi=bi, g=g: e.dma_start(
                                   out=ws2[bi].rearrange("p (c e) -> p c e", e=128)[:, g * 16:(g + 1) * 16, :],
                                   in_=dnqT.rearrange("(c p) e -> p c e", p=128)[:, g * 16:(g + 1) * 16, s * 128:(s + 1) * 128]),
                               deps=[w2_war[bi]]))
                bank = s % 2
                mm = None
                for fc in range(FCH):
                    mm = S.op("tensor", lambda t, fc=fc, bank=bank, bi=bi: t.matmul(
                        ps_a[:, bank * 512:bank * 512 + 512],
                        ws2[bi][:, fc * 128:(fc + 1) * 128],
                        hsb[:, fc * 512:(fc + 1) * 512],
                        start=(fc == 0), stop=(fc == FCH - 1)),
                        ([h_done_s, pa.last[bank]] + lds2 if fc == 0 else []))
                w2_war[bi] = mm
                ev = S.op("scalar", lambda s_, bank=bank, s=s: s_.activation(
                    out=yf[:, :], in_=ps_a[:, bank * 512:bank * 512 + 512],
                    func=AF.Identity, scale=gmsc[:, s:s + 1]), [mm, gsc, yf_war])
                pa.use(bank, ev)
                for it in range(4):
                    tb = 2 + (it % 2)
                    tr = S.op("tensor", lambda t, it=it, tb=tb: t.transpose(
                        ps_a[:, tb * 512:tb * 512 + 128], yf[:, it * 128:(it + 1) * 128],
                        identf[:]), [ev, pa.last[tb], idtf])
                    add = S.op("vector", lambda v, it=it, tb=tb, s=s: v.tensor_tensor(
                        out=xt[it][:, s * 128:(s + 1) * 128],
                        in0=ps_a[:, tb * 512:tb * 512 + 128],
                        in1=xt[it][:, s * 128:(s + 1) * 128], op=ALU.add), [tr, xt_war[it]])
                    pa.use(tb, add)
                    xt_war[it] = add
                    yf_war = tr
            sts = []
            for it in range(4):
                st = S.dma("sync", "sy%d" % (it % 2),
                           lambda e, it=it, th0=th0: e.dma_start(
                               out=y[th0 + it * 128: th0 + (it + 1) * 128, :], in_=xt[it][:]),
                           deps=[xt_war[it]])
                xt_war[it] = st
                sts.append(st)
        S.wait("sync", sts)
        S.finish(stack)
    return nc



def build_attnb2():
    nc = bass.Bass("TRN2", target_bir_lowering=False)
    x = nc.dram_tensor("x", [TPC, D], F32, kind="ExternalInput").ap()
    wqkv = nc.dram_tensor("wqkv", [D, 3 * D], F32, kind="ExternalInput").ap()
    wout = nc.dram_tensor("wout", [D, D], F32, kind="ExternalInput").ap()
    nw = nc.dram_tensor("nw", [1, D], F32, kind="ExternalInput").ap()
    pmk = nc.dram_tensor("pmk", [D, 16], F32, kind="ExternalInput").ap()
    pmv = nc.dram_tensor("pmv", [16, D], F32, kind="ExternalInput").ap()
    y = nc.dram_tensor("y", [TPC, D], F32, kind="ExternalOutput").ap()
    EPS32 = float(2.0 ** -23)
    SCL = 0.125
    TH = 512

    S = Sched(nc)
    with contextlib.ExitStack() as stack:
        sb = lambda n, sh, dt=F32: stack.enter_context(nc.sbuf_tensor(n, sh, dt))
        xt = [sb("xt%d" % i, [128, D]) for i in range(4)]
        xn = sb("xn", [128, D])
        xn2 = sb("xn2", [128, D])
        xn2b = sb("xn2b", [128, D], BF16)
        st6 = sb("st6", [128, 24]); st2 = sb("st2", [128, 2])
        rstd = sb("rstd", [128, 1]); mbias = sb("mbias", [128, 1])
        ms2 = sb("ms2", [128, 1]); r2 = sb("r2", [128, 1])
        ones = sb("ones", [128, 128], BF16); ident = sb("ident", [128, 128], BF16)
        onesf = sb("onesf", [128, 128]); identf = sb("identf", [128, 128])
        nws = sb("nws", [1, D]); nwb = sb("nwb", [128, D])
        xnT = sb("xnT", [128, DCH * TH], BF16)
        qfm = sb("qfm", [128, DCH * TH], BF16)
        kfm = sb("kfm", [128, DCH * TH], BF16)
        vtok = sb("vtok", [128, 4 * D], BF16)
        wsl = [sb("wsl%d" % i, [128, DCH * 512], BF16) for i in range(2)]
        pmks = sb("pmks", [128, 16 * 16], BF16)
        pmvs = sb("pmvs", [16, D], BF16)
        ones4 = sb("ones4", [4, 128])
        g4 = sb("g4", [4, 1])
        pr = sb("pr", [128, 1]); jr = sb("jr", [128, 1])
        kk1 = sb("kk1", [1, 16])
        gs = sb("gs", [128, 128])
        maskall = sb("maskall", [128, 144])
        ssb = sb("ssb", [128, 144])
        pb = sb("pb", [128, 144])
        pbb = sb("pbb", [128, 144], BF16)
        ptm = sb("ptm", [16, 128], BF16)
        pts = sb("pts", [128, 128], BF16)
        mx = sb("mx", [128, 1]); nmx = sb("nmx", [128, 1])
        sm = sb("sm", [128, 1]); rs = sb("rs", [128, 1])
        ytok = sb("ytok", [128, D])
        yT = sb("yT", [128, DCH * TH], BF16)
        ybst = [sb("ybst%d" % i, [128, 512]) for i in range(2)]
        ps_t = stack.enter_context(nc.psum_tensor("ps_t", [128, 2048], BF16))
        ps_a = stack.enter_context(nc.psum_tensor("ps_a", [128, 2048], F32))

        idt = emit_identity(S, ones[:], ident[:])
        idtf = emit_identity(S, onesf[:], identf[:])
        lnw = S.dma("sync", "lc", lambda e: e.dma_start(out=nws[:], in_=nw[:, :]))
        lpk = S.dma("gpsimd", "lp", lambda e: e.dma_start(
            out=pmks.rearrange("p (c e) -> p c e", e=16),
            in_=pmk.rearrange("(c p) e -> p c e", p=128)))
        lpv = S.dma("gpsimd", "lp", lambda e: e.dma_start(out=pmvs[:], in_=pmv[:, :]))

        bn_ms = []
        for i in range(4):
            bm = S.op("tensor", lambda t, i=i: t.matmul(
                ps_a[:, i * 512:(i + 1) * 512], onesf[0:1, :], nws[0:1, i * 512:(i + 1) * 512],
                start=True, stop=True), [lnw, idtf])
            ev = S.op("vector", lambda v, i=i: v.tensor_copy(
                nwb[:, i * 512:(i + 1) * 512], ps_a[:, i * 512:(i + 1) * 512]), [bm])
            bn_ms.append(ev)

        o4 = S.op("vector", lambda v: v.memset(ones4[:], 1.0))
        rt1 = S.op("gpsimd", lambda g: g.affine_select(
            out=ones4[:], in_=ones4[:], pattern=[[1, 128]], compare_op=ALU.is_ge,
            fill=0.0, base=0, channel_multiplier=-32), [o4])
        rt2 = S.op("gpsimd", lambda g: g.affine_select(
            out=ones4[:], in_=ones4[:], pattern=[[-1, 128]], compare_op=ALU.is_ge,
            fill=0.0, base=31, channel_multiplier=32), [rt1])
        gmm = S.op("tensor", lambda t: t.matmul(
            ps_a[:, 0:128], ones4[:], ones4[:], start=True, stop=True),
            [rt2, bn_ms[0]])
        gev = S.op("vector", lambda v: v.tensor_copy(gs[:], ps_a[:, 0:128]), [gmm])
        mq0 = S.op("vector", lambda v: v.memset(maskall[:, 16:144], 0.0))
        mq1 = S.op("gpsimd", lambda g: g.affine_select(
            out=maskall[:, 16:144], in_=maskall[:, 16:144], pattern=[[-1, 128]],
            compare_op=ALU.is_ge, fill=-1e38, base=-16, channel_multiplier=1), [mq0])
        mq2 = S.op("gpsimd", lambda g: g.affine_select(
            out=maskall[:, 16:144], in_=maskall[:, 16:144], pattern=[[1, 128]],
            compare_op=ALU.is_ge, fill=-1e38, base=31, channel_multiplier=-1), [mq1])
        tg = S.op("vector", lambda v: v.tensor_scalar(
            out=gs[:], in0=gs[:], scalar1=1e38, scalar2=-1e38, op0=ALU.mult, op1=ALU.add),
            [gev])
        mq3 = S.op("vector", lambda v: v.tensor_tensor(
            out=maskall[:, 16:144], in0=maskall[:, 16:144], in1=gs[:], op=ALU.add),
            [mq2, tg])
        it1 = S.op("gpsimd", lambda g: g.iota(pr[:], pattern=[[0, 1]], base=0, channel_multiplier=1, allow_small_or_imprecise_dtypes=True))
        it2 = S.op("gpsimd", lambda g: g.iota(g4[:], pattern=[[0, 1]], base=0, channel_multiplier=1, allow_small_or_imprecise_dtypes=True))
        it3 = S.op("gpsimd", lambda g: g.iota(kk1[:], pattern=[[1, 16]], base=0, channel_multiplier=0, allow_small_or_imprecise_dtypes=True))
        gvm = S.op("tensor", lambda t: t.matmul(
            ps_a[:, 512:513], ones4[:], g4[:], start=True, stop=True),
            [rt2, it2, bn_ms[1]])
        gve = S.op("vector", lambda v: v.tensor_copy(jr[:], ps_a[:, 512:513]), [gvm])
        jr1 = S.op("vector", lambda v: v.scalar_tensor_tensor(
            out=jr[:], in0=jr[:], scalar=-32.0, in1=pr[:], op0=ALU.mult, op1=ALU.add),
            [gve, it1])
        kmm = S.op("tensor", lambda t: t.matmul(
            ps_a[:, 1024:1040], onesf[0:1, :], kk1[0:1, :], start=True, stop=True),
            [it3, idtf, bn_ms[2]])
        kev = S.op("vector", lambda v: v.tensor_copy(maskall[:, 0:16], ps_a[:, 1024:1040]), [kmm])
        dmk = S.op("vector", lambda v: v.tensor_scalar(
            out=maskall[:, 0:16], in0=maskall[:, 0:16], scalar1=jr[:, 0:1], scalar2=None,
            op0=ALU.is_le), [kev, jr1])
        dm2 = S.op("vector", lambda v: v.tensor_scalar(
            out=maskall[:, 0:16], in0=maskall[:, 0:16], scalar1=1e38, scalar2=-1e38,
            op0=ALU.mult, op1=ALU.add), [dmk])
        mask_done = [mq3, dm2]

        pa = PsumBanks(4)
        pa.use(0, gev); pa.use(1, gve); pa.use(2, kev); pa.use(3, bn_ms[3])
        tp_last = [None, None]
        w_war = [None, None]
        xt_war = [None] * 4
        sts = []
        xn_war = None
        for half in range(2):
            th0 = half * TH
            lnlast = None
            evs = None
            for it in range(4):
                ld = S.dma("sync", "lx%d" % (it % 2),
                           lambda e, it=it, th0=th0: e.dma_start(
                               out=xt[it][:], in_=x[th0 + it * 128: th0 + (it + 1) * 128, :]),
                           deps=[xt_war[it]])
                lnt, _ = emit_ln(S, xt[it][:], xn[:], st6[:], st2[:], rstd[:], mbias[:],
                                 [ld, lnlast, xn_war])
                lnlast = lnt
                b1 = []
                for i4 in range(4):
                    b1.append(S.op("vector", lambda v, i4=i4: v.bn_stats(
                        st6[:, 6 * i4:6 * (i4 + 1)], xn[:, i4 * 512:(i4 + 1) * 512]), [lnt]))
                b2 = S.op("vector", lambda v: v.bn_aggr(st2[:], st6[:]), b1)
                m1 = S.op("vector", lambda v: v.tensor_tensor(
                    out=ms2[:], in0=st2[:, 0:1], in1=st2[:, 0:1], op=ALU.mult), [b2])
                m2 = S.op("vector", lambda v: v.tensor_tensor(
                    out=ms2[:], in0=ms2[:], in1=st2[:, 1:2], op=ALU.add), [m1])
                m3 = S.op("vector", lambda v: v.tensor_scalar(
                    out=r2[:], in0=ms2[:], scalar1=EPS32, scalar2=None, op0=ALU.add), [m2])
                m4 = S.op("scalar", lambda s_: s_.activation(out=r2[:], in_=r2[:], func=AF.Sqrt), [m3])
                m5 = S.op("vector", lambda v: v.reciprocal(out=r2[:], in_=r2[:]), [m4])
                x2 = S.op("vector", lambda v: v.scalar_tensor_tensor(
                    out=xn2[:], in0=xn[:], scalar=r2[:, 0:1], in1=nwb[:],
                    op0=ALU.mult, op1=ALU.mult), [m5] + bn_ms)
                xn_war = x2
                xb = S.op("vector", lambda v: v.tensor_copy(xn2b[:], xn2[:]), [x2])
                evs = []
                for c in range(DCH):
                    bank = c % 2
                    tr = S.op("tensor", lambda t, c=c, bank=bank: t.transpose(
                        ps_t[:, bank * 1024:bank * 1024 + 128], xn2b[:, c * 128:(c + 1) * 128],
                        ident[:]), [xb, idt, tp_last[bank]])
                    dst = xnT[:, c * TH + it * 128: c * TH + (it + 1) * 128]
                    if c % 2 == 0:
                        ev = S.op("vector", lambda v, dst=dst, bank=bank: v.tensor_copy(
                            dst, ps_t[:, bank * 1024:bank * 1024 + 128]), [tr])
                    else:
                        ev = S.op("scalar", lambda s_, dst=dst, bank=bank: s_.copy(
                            dst, ps_t[:, bank * 1024:bank * 1024 + 128]), [tr])
                    tp_last[bank] = ev
                    evs.append(ev)
                lnlast = evs[-1]
            xnT_all = [evs[-2], evs[-1]]

            for s in range(12):
                bi = s % 2
                ld = S.dma("gpsimd", "lw%d" % bi,
                           lambda e, s=s, bi=bi: e.dma_start(
                               out=wsl[bi].rearrange("p (c e) -> p c e", e=512),
                               in_=wqkv.rearrange("(c p) e -> p c e", p=128)[:, :, s * 512:(s + 1) * 512]),
                           deps=[w_war[bi]])
                lastop = None
                if s < 8:
                    dstb = qfm if s < 4 else kfm
                    for ec in range(4):
                        e_glob = (s % 4) * 4 + ec
                        bank = ec % 4
                        mm = None
                        for dc in range(DCH):
                            mm = S.op("tensor", lambda t, dc=dc, ec=ec, bank=bank, bi=bi: t.matmul(
                                ps_a[:, bank * 512:bank * 512 + TH],
                                wsl[bi][:, dc * 512 + ec * 128: dc * 512 + ec * 128 + 128],
                                xnT[:, dc * TH:(dc + 1) * TH],
                                start=(dc == 0), stop=(dc == DCH - 1)),
                                ([ld, pa.last[bank]] + xnT_all if dc == 0 else []))
                        dst = dstb[:, e_glob * TH:(e_glob + 1) * TH]
                        ev = S.op("scalar", lambda s_, dst=dst, bank=bank: s_.activation(
                            out=dst, in_=ps_a[:, bank * 512:bank * 512 + TH],
                            func=AF.Identity), [mm])
                        pa.use(bank, ev)
                        lastop = ev
                else:
                    e0 = (s - 8) * 512
                    for it in range(4):
                        bank = it % 4
                        mm = None
                        for dc in range(DCH):
                            mm = S.op("tensor", lambda t, dc=dc, it=it, bank=bank, bi=bi: t.matmul(
                                ps_a[:, bank * 512:bank * 512 + 512],
                                xnT[:, dc * TH + it * 128: dc * TH + (it + 1) * 128],
                                wsl[bi][:, dc * 512:(dc + 1) * 512],
                                start=(dc == 0), stop=(dc == DCH - 1)),
                                ([ld, pa.last[bank]] + xnT_all if dc == 0 else []))
                        dst = vtok[:, it * D + e0: it * D + e0 + 512]
                        ev = S.op("scalar", lambda s_, dst=dst, bank=bank: s_.activation(
                            out=dst, in_=ps_a[:, bank * 512:bank * 512 + 512],
                            func=AF.Identity), [mm])
                        pa.use(bank, ev)
                        lastop = ev
                w_war[bi] = lastop

            proj_done = [w_war[0], w_war[1]]
            ssb_war = None
            pb_war = None
            pbb_war = None
            ptm_war = None
            pts_war = None
            ytok_war = None
            stat_war = None
            yT_done = None
            for it in range(4):
                ytok_evs = []
                for h in range(32):
                    pb_ = (h % 2) * 64
                    hc = h // 2
                    bs = (h % 2) * 2
                    mmm = S.op("tensor", lambda t, pb_=pb_, hc=hc, it=it, bs=bs: t.matmul(
                        ps_a[:, bs * 512:bs * 512 + 16],
                        qfm[pb_:pb_ + 64, hc * TH + it * 128: hc * TH + (it + 1) * 128],
                        pmks[pb_:pb_ + 64, hc * 16:(hc + 1) * 16],
                        start=True, stop=True),
                        ([pa.last[bs], lpk] + proj_done) if True else [])
                    mms = S.op("tensor", lambda t, pb_=pb_, hc=hc, it=it, bs=bs: t.matmul(
                        ps_a[:, (bs + 1) * 512:(bs + 1) * 512 + 128],
                        qfm[pb_:pb_ + 64, hc * TH + it * 128: hc * TH + (it + 1) * 128],
                        kfm[pb_:pb_ + 64, hc * TH + it * 128: hc * TH + (it + 1) * 128],
                        start=True, stop=True), [pa.last[bs + 1]])
                    ev1 = S.op("scalar", lambda s_, bs=bs: s_.activation(
                        out=ssb[:, 0:16], in_=ps_a[:, bs * 512:bs * 512 + 16],
                        func=AF.Identity, scale=SCL), [mmm, ssb_war])
                    ev2 = S.op("scalar", lambda s_, bs=bs: s_.activation(
                        out=ssb[:, 16:144], in_=ps_a[:, (bs + 1) * 512:(bs + 1) * 512 + 128],
                        func=AF.Identity, scale=SCL), [mms])
                    pa.use(bs, ev1)
                    pa.use(bs + 1, ev2)
                    madd = S.op("vector", lambda v: v.tensor_tensor(
                        out=ssb[:], in0=ssb[:], in1=maskall[:], op=ALU.add),
                        [ev1, ev2] + mask_done)
                    rmx = S.op("vector", lambda v: v.reduce_max(
                        out=mx[:], in_=ssb[:], axis=AX.X), [madd, stat_war])
                    ngx = S.op("vector", lambda v: v.tensor_scalar(
                        out=nmx[:], in0=mx[:], scalar1=-1.0, scalar2=None, op0=ALU.mult), [rmx])
                    ex = S.op("scalar", lambda s_: s_.activation(
                        out=pb[:], in_=ssb[:], func=AF.Exp, bias=nmx[:, 0:1]),
                        [ngx, pb_war])
                    ssb_war = ex
                    sumr = S.op("vector", lambda v: v.reduce_sum(
                        out=sm[:], in_=pb[:], axis=AX.X), [ex])
                    rcp = S.op("vector", lambda v: v.reciprocal(out=rs[:], in_=sm[:]), [sumr])
                    pnm = S.op("vector", lambda v: v.tensor_scalar(
                        out=pbb[:], in0=pb[:], scalar1=rs[:, 0:1], scalar2=None,
                        op0=ALU.mult), [rcp, pbb_war])
                    pb_war = pnm
                    stat_war = pnm
                    tr1 = S.op("tensor", lambda t: t.transpose(
                        ps_t[:, 0:128], pbb[:, 16:144], ident[:]), [pnm, idt, pts_war])
                    pte = S.op("vector", lambda v: v.tensor_copy(pts[:], ps_t[:, 0:128]), [tr1])
                    tr2 = S.op("tensor", lambda t: t.transpose(
                        ps_t[0:16, 1024:1152], pbb[:, 0:16], ident[:]), [pnm, idt, ptm_war])
                    pme = S.op("vector", lambda v: v.tensor_copy(ptm[:], ps_t[0:16, 1024:1152]), [tr2])
                    pbb_war = tr2
                    pvb = 2 if (h % 2) == 0 else 0
                    mv1 = S.op("tensor", lambda t, h=h, pvb=pvb: t.matmul(
                        ps_a[:, pvb * 512 + 256: pvb * 512 + 256 + 64],
                        ptm[:], pmvs[:, h * 64:(h + 1) * 64],
                        start=True, stop=False), [pme, lpv, pa.last[pvb]])
                    mv2 = S.op("tensor", lambda t, h=h, it=it, pvb=pvb: t.matmul(
                        ps_a[:, pvb * 512 + 256: pvb * 512 + 256 + 64],
                        pts[:], vtok[:, it * D + h * 64: it * D + (h + 1) * 64],
                        start=False, stop=True), [pte])
                    pts_war = mv2
                    ptm_war = mv2
                    oev = S.op("scalar", lambda s_, h=h, pvb=pvb: s_.activation(
                        out=ytok[:, h * 64:(h + 1) * 64],
                        in_=ps_a[:, pvb * 512 + 256: pvb * 512 + 256 + 64],
                        func=AF.Identity), [mv2, ytok_war])
                    pa.use(pvb, oev)
                    ytok_evs.append(oev)
                ytev = []
                for c in range(DCH):
                    bank = c % 2
                    trc = S.op("tensor", lambda t, c=c, bank=bank: t.transpose(
                        ps_a[:, bank * 512 + 128:bank * 512 + 256].bitcast(F32) if False else ps_a[:, bank * 512 + 128:bank * 512 + 256],
                        ytok[:, c * 128:(c + 1) * 128], identf[:]),
                        ytok_evs + [idtf, pa.last[bank]])
                    dst = yT[:, c * TH + it * 128: c * TH + (it + 1) * 128]
                    evc = S.op("vector", lambda v, dst=dst, bank=bank: v.tensor_copy(
                        dst, ps_a[:, bank * 512 + 128:bank * 512 + 256]), [trc])
                    pa.use(bank, evc)
                    ytev.append(evc)
                ytok_war = ytev[-1]
                yT_done = ytev[-1]

            wslabs4 = [qfm[:, 0:8192], kfm[:, 0:8192], vtok[:, 0:8192], wsl[0][:, 0:8192]]
            wld4 = []
            for s4 in range(4):
                ld4 = S.dma("gpsimd", "lo%d" % (s4 % 2),
                            lambda e, s4=s4: e.dma_start(
                                out=wslabs4[s4].rearrange("p (c e) -> p c e", e=512),
                                in_=wout.rearrange("(c p) e -> p c e", p=128)[:, :, s4 * 512:(s4 + 1) * 512]),
                            deps=[yT_done])
                wld4.append(ld4)
            adds = []
            for it in range(4):
                for s4 in range(4):
                    bank = s4 % 4
                    yb = s4 % 2
                    mm = None
                    for dc in range(DCH):
                        mm = S.op("tensor", lambda t, dc=dc, it=it, bank=bank, s4=s4: t.matmul(
                            ps_a[:, bank * 512:bank * 512 + 512],
                            yT[:, dc * TH + it * 128: dc * TH + (it + 1) * 128],
                            wslabs4[s4][:, dc * 512:(dc + 1) * 512],
                            start=(dc == 0), stop=(dc == DCH - 1)),
                            ([wld4[s4], pa.last[bank], yT_done] if dc == 0 else []))
                    add = S.op("vector", lambda v, it=it, s4=s4, bank=bank, yb=yb: v.tensor_tensor(
                        out=ybst[yb][:], in0=ps_a[:, bank * 512:bank * 512 + 512],
                        in1=xt[it][:, s4 * 512:(s4 + 1) * 512], op=ALU.add),
                        [mm] + ([sts[-2]] if len(sts) >= 2 else []))
                    pa.use(bank, add)
                    adds.append(add)
                    st = S.dma("sync", "sy%d" % yb,
                               lambda e, it=it, s4=s4, yb=yb, th0=th0: e.dma_start(
                                   out=y[th0 + it * 128: th0 + (it + 1) * 128, s4 * 512:(s4 + 1) * 512],
                                   in_=ybst[yb][:]), deps=[add])
                    sts.append(st)
                xt_war[it] = adds[-1]
            w_war = [sts[-1], sts[-2]]
        S.wait("sync", sts[-2:])
        S.finish(stack)
    return nc


def _cc(a):
    return np.ascontiguousarray(a, dtype=np.float32)


def _rows(a, c, n=NCORES):
    r = a.shape[0] // n
    return _cc(a[c * r:(c + 1) * r])


def kernel(L0_attn_a_x, L0_attn_a_w_attn, L0_attn_a_w_o,
           L0_attn_b_Wq, L0_attn_b_Wk, L0_attn_b_Wv, L0_attn_b_Wo,
           L0_ffn_w1, L0_ffn_b1, L0_ffn_w2, L0_ffn_b2,
           L1_attn_a_Wq, L1_attn_a_Wk, L1_attn_a_Wv, L1_attn_a_Wo,
           L1_attn_b_w_norm, L1_attn_b_w_qkv, L1_attn_b_w_out, L1_attn_b_pm,
           L1_ffn_weight_up, L1_ffn_weight_down, L1_ffn_norm_weight, L1_ffn_gamma):
    cores = list(range(NCORES))
    x = _cc(np.asarray(L0_attn_a_x)).reshape(TOK, D)

    w1T = _cc(np.asarray(L0_ffn_w1).T)
    w2T = _cc(np.asarray(L0_ffn_w2).T)
    upT = _cc(np.asarray(L1_ffn_weight_up).T)
    dnT = _cc(np.asarray(L1_ffn_weight_down).T)
    waT = _cc(np.asarray(L0_attn_a_w_attn).T)
    woT = _cc(np.asarray(L0_attn_a_w_o).T)
    bqT = _cc(np.asarray(L0_attn_b_Wq).T)
    bkT = _cc(np.asarray(L0_attn_b_Wk).T)
    bvT = _cc(np.asarray(L0_attn_b_Wv).T)
    boT = _cc(np.asarray(L0_attn_b_Wo).T)
    b1T = _cc(np.asarray(L0_ffn_b1).reshape(64, 128).T)
    b2T = _cc(np.asarray(L0_ffn_b2).reshape(16, 128).T)
    aWq = _cc(np.asarray(L1_attn_a_Wq)); aWk = _cc(np.asarray(L1_attn_a_Wk))
    aWv = _cc(np.asarray(L1_attn_a_Wv)); aWo = _cc(np.asarray(L1_attn_a_Wo))
    nw1 = _cc(np.asarray(L1_ffn_norm_weight).reshape(1, D))
    gmT = _cc(np.asarray(L1_ffn_gamma).reshape(16, 128).T)
    nw2 = _cc(np.asarray(L1_attn_b_w_norm).reshape(1, D))
    wqkv = _cc(np.asarray(L1_attn_b_w_qkv))
    woutB = _cc(np.asarray(L1_attn_b_w_out))
    pm = np.asarray(L1_attn_b_pm)
    pmkT = _cc(pm[0].transpose(0, 2, 1).reshape(D, 16))
    pmvL = _cc(pm[1].transpose(1, 0, 2).reshape(16, D))

    nc = build_qscale()
    im = [{"w1s": _rows(w1T, c), "ups": _rows(upT, c), "w2s": _rows(w2T, c),
           "dns": _rows(dnT, c)} for c in cores]
    r = run_bass_kernel_spmd(nc, im, core_ids=cores)
    psums = np.concatenate([r.results[c]["sums"] for c in cores], axis=0)

    nc = build_quant()
    im = [{"w1s": _rows(w1T, c), "ups": _rows(upT, c), "w2s": _rows(w2T, c),
           "dns": _rows(dnT, c), "psums": psums} for c in cores]
    r = run_bass_kernel_spmd(nc, im, core_ids=cores)
    scales = _cc(r.results[0]["scales"])
    qw1T = np.concatenate([r.results[c]["qw1"] for c in cores], axis=0)
    qupT = np.concatenate([r.results[c]["qup"] for c in cores], axis=0)
    qw2T = np.concatenate([r.results[c]["qw2"] for c in cores], axis=0)
    qdnT = np.concatenate([r.results[c]["qdn"] for c in cores], axis=0)

    nc = build_attn_a()
    im = []
    for c in cores:
        t0 = c * TPC
        if c % 4 == 0:
            xh = np.concatenate([x[t0:t0 + 128], x[t0:t0 + TPC]], axis=0)
            fl = np.zeros((128, 1), np.float32)
        else:
            xh = x[t0 - 128:t0 + TPC]
            fl = np.ones((128, 1), np.float32)
        im.append({"xh": _cc(xh), "flag": fl, "waT": waT, "woT": woT})
    r = run_bass_kernel_spmd(nc, im, core_ids=cores)
    x = np.concatenate([r.results[c]["y"] for c in cores], axis=0)

    nc_ffn0 = build_ffn0()
    im = [{"x": _rows(x, c), "w1qT": qw1T, "b1T": b1T, "w2qT": qw2T, "b2T": b2T}
          for c in cores]
    r = run_bass_kernel_spmd(nc_ffn0, im, core_ids=cores)
    x = np.concatenate([r.results[c]["y"] for c in cores], axis=0)

    nc = build_attnb_qkv()
    im = [{"x": _rows(x, c), "wqT": bqT, "wkT": bkT, "wvT": bvT} for c in cores]
    r = run_bass_kernel_spmd(nc, im, core_ids=cores)
    qs = np.concatenate([r.results[c]["qs"] for c in cores], 0).reshape(B, T, D)
    ks = np.concatenate([r.results[c]["ks"] for c in cores], 0).reshape(B, T, D)
    vs = np.concatenate([r.results[c]["vs"] for c in cores], 0).reshape(B, T, D)
    nc = build_attnb_scan()
    im = []
    for c in cores:
        dmap = {}
        for hh in range(4):
            p = c * 4 + hh
            bb, h = p // 16, p % 16
            dmap["qf%d" % hh] = _cc(qs[bb, :, h * 128:(h + 1) * 128].T)
            dmap["kf%d" % hh] = _cc(ks[bb, :, h * 128:(h + 1) * 128].T)
            dmap["kt%d" % hh] = _cc(ks[bb, :, h * 128:(h + 1) * 128])
            dmap["vt%d" % hh] = _cc(vs[bb, :, h * 128:(h + 1) * 128])
        im.append(dmap)
    r = run_bass_kernel_spmd(nc, im, core_ids=cores)
    att = np.zeros((B, T, D), np.float32)
    for c in cores:
        for hh in range(4):
            p = c * 4 + hh
            bb, h = p // 16, p % 16
            att[bb, :, h * 128:(h + 1) * 128] = r.results[c]["o%d" % hh]
    attf = att.reshape(TOK, D)
    nc_pr = build_proj_residual()
    im = [{"x": _rows(x, c), "aT": _cc(attf[c * TPC:(c + 1) * TPC].T), "m": boT}
          for c in cores]
    r = run_bass_kernel_spmd(nc_pr, im, core_ids=cores)
    x = np.concatenate([r.results[c]["y"] for c in cores], axis=0)

    im = [{"x": _rows(x, c), "w1qT": qw1T, "b1T": b1T, "w2qT": qw2T, "b2T": b2T}
          for c in cores]
    r = run_bass_kernel_spmd(nc_ffn0, im, core_ids=cores)
    x = np.concatenate([r.results[c]["y"] for c in cores], axis=0)

    nc = build_attnb_qkv(binarize=False, qscale=float(128.0 ** -0.5))
    im = [{"x": _rows(x, c), "wqT": aWq, "wkT": aWk, "wvT": aWv} for c in cores]
    r = run_bass_kernel_spmd(nc, im, core_ids=cores)
    qs = np.concatenate([r.results[c]["qs"] for c in cores], 0).reshape(B, T, D)
    ks = np.concatenate([r.results[c]["ks"] for c in cores], 0).reshape(B, T, D)
    vs = np.concatenate([r.results[c]["vs"] for c in cores], 0).reshape(B, T, D)
    nc = build_attna_causal()
    im = []
    for c in cores:
        dmap = {}
        for hh in range(4):
            p = c * 4 + hh
            bb, h = p // 16, p % 16
            dmap["qf%d" % hh] = _cc(qs[bb, :, h * 128:(h + 1) * 128].T)
            dmap["kf%d" % hh] = _cc(ks[bb, :, h * 128:(h + 1) * 128].T)
            dmap["vt%d" % hh] = _cc(vs[bb, :, h * 128:(h + 1) * 128])
        im.append(dmap)
    r = run_bass_kernel_spmd(nc, im, core_ids=cores)
    att = np.zeros((B, T, D), np.float32)
    for c in cores:
        for hh in range(4):
            p = c * 4 + hh
            bb, h = p // 16, p % 16
            att[bb, :, h * 128:(h + 1) * 128] = r.results[c]["o%d" % hh]
    attf = att.reshape(TOK, D)
    im = [{"x": _rows(x, c), "aT": _cc(attf[c * TPC:(c + 1) * TPC].T), "m": aWo}
          for c in cores]
    r = run_bass_kernel_spmd(nc_pr, im, core_ids=cores)
    x = np.concatenate([r.results[c]["y"] for c in cores], axis=0)

    nc_ffn1 = build_ffn1()
    im = [{"x": _rows(x, c), "upqT": qupT, "dnqT": qdnT, "nw": nw1, "gmT": gmT,
           "scl": scales} for c in cores]
    r = run_bass_kernel_spmd(nc_ffn1, im, core_ids=cores)
    x = np.concatenate([r.results[c]["y"] for c in cores], axis=0)

    nc = build_attnb2()
    im = [{"x": _rows(x, c), "wqkv": wqkv, "wout": woutB, "nw": nw2,
           "pmk": pmkT, "pmv": pmvL} for c in cores]
    r = run_bass_kernel_spmd(nc, im, core_ids=cores)
    x = np.concatenate([r.results[c]["y"] for c in cores], axis=0)

    im = [{"x": _rows(x, c), "upqT": qupT, "dnqT": qdnT, "nw": nw1, "gmT": gmT,
           "scl": scales} for c in cores]
    r = run_bass_kernel_spmd(nc_ffn1, im, core_ids=cores)
    x = np.concatenate([r.results[c]["y"] for c in cores], axis=0)

    return x.reshape(B, T, D)
